# version 02
# baseline (speedup 1.0000x reference)
import contextlib
import numpy as np
import ml_dtypes
import concourse.bass as bass
import concourse.mybir as mybir
from concourse.bass_utils import run_bass_kernel_spmd

F32 = mybir.dt.float32
BF16 = mybir.dt.bfloat16
AF = mybir.ActivationFunctionType
ALU = mybir.AluOpType

D = 1024
DP = 4256
DI = 2048
L, SEG = 8192, 4096
CH = 128
HS = 256
NHS = SEG // HS
NCHUNK = SEG // CH
FH = 4096
XN_STR = SEG + 256
NEG = -1e30

ENGS = ("tensor", "vector", "scalar", "sync", "gpsimd")

DEBUG_TAPS = {}
HALO_ZERO_HACK = False
TAP_CHUNK = 0
TAP_Q = 0
TAP_PASS = 1
TAP_HS = 0
HS_LIMIT = None

FCONSTS = [-1.0, 1.0, 0.0, 1e-5, 1.0 / D, -1.0 / D, 1.0 / DI, 1.0 / FH]


class Buf:
    __slots__ = ("name", "last_write", "readers")

    def __init__(self, name):
        self.name = name
        self.last_write = None
        self.readers = []


class Op:
    __slots__ = ("eng", "emit", "deps", "kind", "queue", "consumed", "sem", "val",
                 "fence", "eidx", "waits")

    def __init__(self, eng, emit, kind, queue):
        self.eng, self.emit, self.kind, self.queue = eng, emit, kind, queue
        self.deps = []
        self.consumed = False
        self.sem = None
        self.val = None
        self.fence = False


class Sched:
    def __init__(self):
        self.eng_ops = {e: [] for e in ENGS}
        self.last_dma = {}

    def add(self, eng, emit, reads=(), writes=(), kind="comp", fence=False):
        queue = eng if kind == "dma" else None
        op = Op(eng, emit, kind, queue)
        op.fence = fence
        deps, seen = [], set()
        for b in reads:
            if b.last_write is not None and id(b.last_write) not in seen:
                seen.add(id(b.last_write))
                deps.append(b.last_write)
        for b in writes:
            if b.last_write is not None and id(b.last_write) not in seen:
                seen.add(id(b.last_write))
                deps.append(b.last_write)
            for r in b.readers:
                if id(r) not in seen and r is not op:
                    seen.add(id(r))
                    deps.append(r)
        op.deps = deps
        self.eng_ops[eng].append(op)
        for b in reads:
            b.readers.append(op)
        for b in writes:
            b.last_write = op
            b.readers = []
        if kind == "dma":
            self.last_dma.setdefault(eng, []).append(op)
            if len(self.last_dma[eng]) > 8:
                self.last_dma[eng].pop(0)
        return op

    def dma(self, queue, out, in_, reads=(), writes=(), **kw):
        def emit(e, out=out, in_=in_, kw=kw):
            return e.dma_start(out=out, in_=in_, **kw)
        return self.add(queue, emit, reads=reads, writes=writes, kind="dma")

    def barrier(self, bufs):
        lasts = []
        for e in ENGS:
            for o in reversed(self.eng_ops[e]):
                if o.kind == "comp":
                    lasts.append(o)
                    break
        for ops_ in self.last_dma.values():
            lasts.extend(ops_)
        for e in ENGS:
            if self.eng_ops[e] and self.eng_ops[e][-1].kind == "coll":
                lasts.append(self.eng_ops[e][-1])
        for e in ENGS:
            op = Op(e, lambda eng: eng.nop(), "comp", None)
            op.deps = [d for d in lasts if not (d.eng == e and d.kind == "comp")]
            self.eng_ops[e].append(op)
        for b in bufs:
            b.last_write = None
            b.readers = []

    def finalize(self):
        for e in ENGS:
            for i, op in enumerate(self.eng_ops[e]):
                op.eidx = i
        for e in ENGS:
            for op in self.eng_ops[e]:
                if op.kind != "comp" or op.fence:
                    continue
                for d in op.deps:
                    if (d.kind == "comp" and d.eng == op.eng
                            and op.eidx - d.eidx <= 2):
                        op.fence = True
                        break
        for e in ENGS:
            for op in self.eng_ops[e]:
                for d in op.deps:
                    if d.kind == "comp" and d.eng == op.eng and not op.fence:
                        continue
                    d.consumed = True
        counters = {}
        qidx = {}
        for e in ENGS:
            for op in self.eng_ops[e]:
                if op.kind == "dma":
                    i = qidx.get(op.queue, 0)
                    qidx[op.queue] = i + 1
                    op.sem = f"dma_{op.queue}_{i % 8}"
                    counters[op.sem] = counters.get(op.sem, 0) + 16
                    op.val = counters[op.sem]
                elif op.kind == "coll":
                    op.sem = "coll"
                    counters[op.sem] = counters.get(op.sem, 0) + 1
                    op.val = counters[op.sem]
                elif op.consumed:
                    op.sem = e
                    counters[op.sem] = counters.get(op.sem, 0) + 1
                    op.val = counters[op.sem]

    def stream(self, e, eng, sems):
        watermark = {}
        for op in self.eng_ops[e]:
            for d in op.deps:
                if d.kind == "comp" and d.eng == e and not op.fence:
                    continue
                if watermark.get(d.sem, 0) >= d.val:
                    continue
                eng.wait_ge(sems[d.sem], d.val)
                watermark[d.sem] = d.val
            inst = op.emit(eng)
            if op.sem is not None:
                if op.kind == "coll":
                    inst.then_inc(sems[op.sem])
                elif op.kind == "dma":
                    inst.then_inc(sems[op.sem], 16)
                else:
                    inst.then_inc(sems[op.sem], 1)


def build_program():
    nc = bass.Bass("TRN2", target_bir_lowering=False)
    S = Sched()
    allbufs = []

    def mkbuf(name):
        b = Buf(name)
        allbufs.append(b)
        return b

    def din(name, shape, dtype=F32):
        return nc.dram_tensor(name, list(shape), dtype, kind="ExternalInput").ap()

    x_d = din("x", [SEG + 256, D])
    Win_d = {1: din("p1Win", [D, DP]), 2: din("p2Win", [D, DP])}
    Wout_d = {1: din("p1Wout", [DI, D]), 2: din("p2Wout", [DI, D])}
    Wo_d = {1: din("Wo1", [D, D]), 2: din("Wo2", [D, D])}
    wxs_d = {1: din("p1wxs", [128, 64]), 2: din("p2wxs", [128, 64])}
    wBC_d = {1: din("p1wBC", [64, 8]), 2: din("p2wBC", [64, 8])}
    bxs_d = {1: din("p1bxs", [128, 16]), 2: din("p2bxs", [128, 16])}
    bBC_d = {1: din("p1bBC", [64, 2]), 2: din("p2bBC", [64, 2])}
    dtb_d = {1: din("p1dtb", [32, 1]), 2: din("p2dtb", [32, 1])}
    Alog_d = {1: din("p1Alog", [32, 1]), 2: din("p2Alog", [32, 1])}
    D16_d = {1: din("p1D16", [128, 16]), 2: din("p2D16", [128, 16])}
    normw_d = {1: din("p1normw", [128, 16]), 2: din("p2normw", [128, 16])}
    w1t_d = din("w1t", [D, FH])
    w2t_d = din("w2t", [FH, D])
    b1c_d = din("b1c", [128, 32])
    b2c_d = din("b2c", [128, 8])
    boc_d = din("boc", [128, 8])
    ident_d = din("ident", [128, 128])
    MB_d = {1: din("MB1", [128, 128]), 2: din("MB2", [128, 128])}
    fconst_d = din("fconsts", [128, len(FCONSTS)])

    out_d = nc.dram_tensor("out", [SEG, D], F32, kind="ExternalOutput").ap()
    taps = {}
    for tname, (tshape, tdt) in DEBUG_TAPS.items():
        taps[tname] = nc.dram_tensor("tap_" + tname, list(tshape), tdt,
                                     kind="ExternalOutput").ap()

    xn_dram = nc.dram_tensor("xn_dram", [128, 8 * XN_STR], BF16, kind="Internal").ap()
    scd_dram = {p: nc.dram_tensor(f"scd{p}", [96, SEG], F32, kind="Internal").ap()
                for p in (1, 2)}
    u_dram = {p: nc.dram_tensor(f"ud{p}", [32, SEG], F32, kind="Internal").ap()
              for p in (1, 2)}
    dec_dram = {p: nc.dram_tensor(f"dd{p}", [NCHUNK, 32], F32, kind="Internal").ap()
                for p in (1, 2)}
    P_dram = nc.dram_tensor("P_dram", [128, 8 * SEG], BF16, kind="Internal").ap()
    s_dram = nc.dram_tensor("s_dram", [128, 8 * SEG], BF16, kind="Internal").ap()
    alpha_dram = nc.dram_tensor("alpha_dram", [1, FH], F32, kind="Internal").ap()
    coll_in = nc.dram_tensor("coll_in", [64, 2048], F32, kind="Internal").ap()
    coll_out = nc.dram_tensor("coll_out", [64, 2048], F32, kind="Internal").ap()

    with contextlib.ExitStack() as st:
        ent = st.enter_context

        sems = {}
        for e in ENGS:
            sems[e] = ent(nc.semaphore("sem_" + e))
        for q in ("sync", "gpsimd"):
            for ln in range(8):
                sems[f"dma_{q}_{ln}"] = ent(nc.semaphore(f"dsem_{q}{ln}"))
        sems["coll"] = ent(nc.semaphore("csem"))

        id_f = ent(nc.sbuf_tensor("id_f", [128, 128], F32))
        id_b = ent(nc.sbuf_tensor("id_b", [128, 128], BF16))
        ones_col = ent(nc.sbuf_tensor("ones_col", [128, 1], BF16))
        ones_row = ent(nc.sbuf_tensor("ones_row", [1, 128], BF16))
        wxs = {p: ent(nc.sbuf_tensor(f"wxs{p}", [128, 64], F32)) for p in (1, 2)}
        wBC = {p: ent(nc.sbuf_tensor(f"wBC{p}", [64, 8], F32)) for p in (1, 2)}
        bxs = {p: ent(nc.sbuf_tensor(f"bxs{p}", [128, 16], F32)) for p in (1, 2)}
        bBC = {p: ent(nc.sbuf_tensor(f"bBC{p}", [64, 2], F32)) for p in (1, 2)}
        dtbc = {p: ent(nc.sbuf_tensor(f"dtbc{p}", [32, 1], F32)) for p in (1, 2)}
        Acol = {p: ent(nc.sbuf_tensor(f"Acol{p}", [32, 1], F32)) for p in (1, 2)}
        D16 = {p: ent(nc.sbuf_tensor(f"D16_{p}", [128, 16], BF16)) for p in (1, 2)}
        normw = {p: ent(nc.sbuf_tensor(f"normw{p}", [128, 16], F32)) for p in (1, 2)}
        boc_sb = ent(nc.sbuf_tensor("boc_sb", [128, 8], F32))
        states_f = ent(nc.sbuf_tensor("states_f", [64, 2048], F32))
        states_b = ent(nc.sbuf_tensor("states_b", [64, 2048], BF16))

        ps_main = ent(nc.psum_tensor("ps_main", [128, 2048], F32))
        ps_proj = ent(nc.psum_tensor("ps_proj", [128, 512], F32))
        ps_proj2 = ent(nc.psum_tensor("ps_proj2", [128, 512], F32))
        ps_aux = ent(nc.psum_tensor("ps_aux", [128, 512], F32))
        ps_tg = ent(nc.psum_tensor("ps_tg", [128, 512], F32))
        ps_t = ps_tg[:, 0:256].bitcast(BF16)
        ps_gs = ps_tg[:, 256:384]

        B_main = [mkbuf(f"ps_main{i}") for i in range(4)]
        B_proj = [mkbuf("ps_proj"), mkbuf("ps_proj2")]
        B_aux = mkbuf("ps_aux")
        B_t = mkbuf("ps_tg")
        B_gs = B_t
        B_stf = mkbuf("states_f")
        B_stb = mkbuf("states_b")
        B_consts = mkbuf("consts")
        B_xn_dram = mkbuf("xn_dram")
        B_scd = {p: mkbuf(f"scd{p}") for p in (1, 2)}
        B_u = {p: mkbuf(f"u{p}") for p in (1, 2)}
        B_dec = {p: mkbuf(f"dec{p}") for p in (1, 2)}
        B_P_dram = mkbuf("P_dram")
        B_s_dram = mkbuf("s_dram")
        B_coll = mkbuf("coll")
        B_out = mkbuf("out")
        B_alpha_d = mkbuf("alpha_d")

        def comp(eng, emit, reads=(), writes=(), fence=False):
            return S.add(eng, emit, reads=reads, writes=writes, kind="comp",
                         fence=fence)

        def tap(name, ap, rbuf):
            if name in taps:
                S.dma("sync", taps[name], ap, reads=[rbuf])

        drain_scr = ent(nc.sbuf_tensor("drain_scr", [128, 16], F32))

        def drain(eng, tail_ap, rbuf):
            n = tail_ap.shape[0]

            def em(e, tail_ap=tail_ap, n=n):
                if eng == "scalar":
                    return e.copy(drain_scr[0:n, 0:tail_ap.shape[1]], tail_ap)
                return e.tensor_copy(drain_scr[0:n, 0:tail_ap.shape[1]], tail_ap)
            comp(eng, em, reads=[rbuf], writes=[rbuf])

        fconst_sb = ent(nc.sbuf_tensor("fconst_sb", [128, len(FCONSTS)], F32))
        S.dma("sync", fconst_sb[:, :], fconst_d, writes=[B_consts])
        for i, v in enumerate(FCONSTS):
            nc.const_aps.register(F32, v, fconst_sb[:, i:i + 1])
        S.dma("sync", id_f[:, :], ident_d, writes=[B_consts])
        S.dma("sync", boc_sb[:, :], boc_d, writes=[B_consts])
        for p in (1, 2):
            S.dma("sync", wxs[p][:, :], wxs_d[p], writes=[B_consts])
            S.dma("sync", wBC[p][:, :], wBC_d[p], writes=[B_consts])
            S.dma("sync", bxs[p][:, :], bxs_d[p], writes=[B_consts])
            S.dma("sync", bBC[p][:, :], bBC_d[p], writes=[B_consts])
            S.dma("sync", dtbc[p][:, :], dtb_d[p], writes=[B_consts])
            S.dma("sync", normw[p][:, :], normw_d[p], writes=[B_consts])
            S.dma("sync", Acol[p][:, :], Alog_d[p], writes=[B_consts])
            S.dma("gpsimd", D16[p][:, :], D16_d[p], writes=[B_consts])
        comp("vector", lambda e: e.tensor_copy(id_b[:, :], id_f[:, :]),
             reads=[B_consts], writes=[B_consts])
        comp("vector", lambda e: e.memset(ones_col[:, :], 1.0), writes=[B_consts])
        comp("vector", lambda e: e.memset(ones_row[:, :], 1.0), writes=[B_consts])
        for p in (1, 2):
            def em_exp(e, p=p):
                return e.activation(Acol[p][:, :], Acol[p][:, :], AF.Exp)
            comp("scalar", em_exp, reads=[B_consts], writes=[B_consts])

            def em_neg(e, p=p):
                return e.tensor_scalar_mul(Acol[p][:, :], Acol[p][:, :], -1.0)
            comp("vector", em_neg, reads=[B_consts], writes=[B_consts])

        S.barrier(allbufs)

        with contextlib.ExitStack() as pst:
            pent = pst.enter_context
            x_tm = [pent(nc.sbuf_tensor(f"x_tm{i}", [128, D], F32)) for i in range(2)]
            B_x_tm = [mkbuf(f"x_tm{i}") for i in range(2)]
            lnsc = pent(nc.sbuf_tensor("lnsc", [128, D], BF16))
            B_lnsc = mkbuf("lnsc")
            lncol = pent(nc.sbuf_tensor("lncol", [128, 4], F32))
            B_lncol = mkbuf("lncol")
            xn_tm = pent(nc.sbuf_tensor("xn_tm", [128, D], BF16))
            B_xn_tm = mkbuf("xn_tm")
            xn_ch = pent(nc.sbuf_tensor("xn_ch", [128, 8 * 128], BF16))
            B_xn_ch = mkbuf("xn_ch")
            Windt = {p: pent(nc.sbuf_tensor(f"Windt{p}", [128, 8 * 32], BF16))
                     for p in (1, 2)}
            B_windt = mkbuf("windt")
            dtraw = {p: pent(nc.sbuf_tensor(f"dtraw{p}", [32, SEG], F32)) for p in (1, 2)}
            B_dtraw = {p: mkbuf(f"dtraw{p}") for p in (1, 2)}

            for p in (1, 2):
                for k in range(8):
                    src = bass.AP(Win_d[p].tensor, (128 * k) * DP + 4224,
                                  [[DP, 128], [1, 32]])
                    S.dma("gpsimd", Windt[p][:, 32 * k:32 * k + 32], src,
                          writes=[B_windt])

            for ci in range(34):
                par = ci % 2
                xt = x_tm[par]
                bxt = B_x_tm[par]
                S.dma("sync", xt[:, :], x_d[128 * ci:128 * ci + 128, :], writes=[bxt])

                def em_sum(e, xt=xt):
                    return e.activation(lnsc[:, :], xt[:, :], AF.Copy,
                                        accum_out=lncol[:, 0:1])
                comp("scalar", em_sum, reads=[bxt], writes=[B_lnsc, B_lncol])

                def em_nm(e):
                    return e.mul(lncol[:, 1:2], lncol[:, 0:1], -1.0 / D)
                comp("scalar", em_nm, reads=[B_lncol], writes=[B_lncol], fence=True)

                def em_sq(e, xt=xt):
                    return e.activation(lnsc[:, :], xt[:, :], AF.Square,
                                        bias=lncol[:, 1:2], accum_out=lncol[:, 2:3])
                comp("scalar", em_sq, reads=[bxt, B_lncol], writes=[B_lnsc, B_lncol], fence=True)

                def em_sd(e):
                    return e.activation(lncol[:, 3:4], lncol[:, 2:3], AF.Sqrt,
                                        scale=1.0 / D, bias=1e-5)
                comp("scalar", em_sd, reads=[B_lncol], writes=[B_lncol], fence=True)

                def em_rc(e):
                    return e.reciprocal(lncol[:, 3:4], lncol[:, 3:4])
                comp("vector", em_rc, reads=[B_lncol], writes=[B_lncol])

                def em_xn(e, xt=xt):
                    return e.tensor_scalar(xn_tm[:, :], xt[:, :], lncol[:, 1:2],
                                           lncol[:, 3:4], ALU.add, ALU.mult)
                comp("vector", em_xn, reads=[bxt, B_lncol], writes=[B_xn_tm], fence=True)

                for g in range(2):
                    for kk in range(4):
                        k = 4 * g + kk

                        def em_tp(e, k=k, kk=kk):
                            return e.transpose(ps_t[:, 128 * kk:128 * kk + 128],
                                               xn_tm[:, 128 * k:128 * k + 128],
                                               id_b[:, :])
                        comp("tensor", em_tp, reads=[B_xn_tm, B_consts], writes=[B_t])

                    def em_ev(e, g=g):
                        return e.tensor_copy(xn_ch[:, 512 * g:512 * g + 512], ps_t[:, :])
                    comp("vector", em_ev, reads=[B_t], writes=[B_xn_ch])
                drain("vector", xn_ch[:, 1020:1024], B_xn_ch)
                dst = bass.AP(xn_dram.tensor, 128 * ci,
                              [[8 * XN_STR, 128], [XN_STR, 8], [1, 128]])
                S.dma("sync", dst, xn_ch[:, :].rearrange("p (k t) -> p k t", k=8),
                      reads=[B_xn_ch], writes=[B_xn_dram])

                if 1 <= ci <= 32:
                    tloc = 128 * (ci - 1)
                    for p in (1, 2):
                        for k in range(8):
                            def em_mm(e, p=p, k=k):
                                return e.matmul(ps_gs[0:32, :],
                                                Windt[p][:, 32 * k:32 * k + 32],
                                                xn_ch[:, 128 * k:128 * k + 128],
                                                start=(k == 0), stop=(k == 7))
                            comp("tensor", em_mm, reads=[B_windt, B_xn_ch],
                                 writes=[B_gs])

                        def em_dt(e, p=p, tloc=tloc):
                            return e.tensor_copy(dtraw[p][:, tloc:tloc + 128],
                                                 ps_gs[0:32, :])
                        comp("vector", em_dt, reads=[B_gs], writes=[B_dtraw[p]])

            sp_a = pent(nc.sbuf_tensor("sp_a", [32, SEG], F32))
            B_spa = mkbuf("sp_a")
            sp_m = pent(nc.sbuf_tensor("sp_m", [32, SEG], F32))
            B_spm = mkbuf("sp_m")
            csum = pent(nc.sbuf_tensor("csum", [32, SEG + 1], F32))
            B_csum = mkbuf("csum")
            csb = pent(nc.sbuf_tensor("csb", [32, SEG + 1], F32))
            B_csb = mkbuf("csb")
            ncsum = pent(nc.sbuf_tensor("ncsum", [32, SEG + 1], F32))
            B_ncsum = mkbuf("ncsum")
            sc32 = pent(nc.sbuf_tensor("sc32", [32, 128], F32))
            B_sc32 = mkbuf("sc32")
            dec32 = pent(nc.sbuf_tensor("dec32", [32, 32], F32))
            B_dec32 = mkbuf("dec32")

            for p in (1, 2):
                rw, brw = dtraw[p], B_dtraw[p]

                def em1(e, rw=rw, p=p):
                    return e.activation(sp_a[:, :], rw[:, :], AF.Abs, bias=dtbc[p][:, :])
                comp("scalar", em1, reads=[brw, B_consts], writes=[B_spa])

                def em2(e):
                    return e.activation(sp_a[:, :], sp_a[:, :], AF.Exp, scale=-1.0)
                comp("scalar", em2, reads=[B_spa], writes=[B_spa])

                def em3(e):
                    return e.activation(sp_a[:, :], sp_a[:, :], AF.Ln, bias=1.0)
                comp("scalar", em3, reads=[B_spa], writes=[B_spa])

                def em4(e, rw=rw, p=p):
                    return e.tensor_scalar(sp_m[:, :], rw[:, :], dtbc[p][:, :], 0.0,
                                           ALU.add, ALU.max)
                comp("vector", em4, reads=[brw, B_consts], writes=[B_spm])

                def em5(e):
                    return e.tensor_add(sp_m[:, :], sp_m[:, :], sp_a[:, :])
                comp("vector", em5, reads=[B_spa, B_spm], writes=[B_spm])
                drain("vector", sp_m[0:32, SEG - 4:SEG], B_spm)
                S.dma("sync", bass.AP(scd_dram[p].tensor, 32 * SEG, [[SEG, 32], [1, SEG]]),
                      sp_m[:, :], reads=[B_spm], writes=[B_scd[p]])

                def em6(e, p=p):
                    return e.tensor_scalar_mul(sp_a[:, :], sp_m[:, :], Acol[p][:, :])
                comp("vector", em6, reads=[B_spm, B_consts], writes=[B_spa])

                def em7(e):
                    return e.memset(sp_m[:, :], 0.0)
                comp("vector", em7, writes=[B_spm])

                def em8(e):
                    return e.tensor_tensor_scan(csum[:, 1:SEG + 1], sp_a[:, :],
                                                sp_m[:, :], 0.0, ALU.add, ALU.add)
                comp("vector", em8, reads=[B_spa, B_spm], writes=[B_csum])

                def em9(e):
                    return e.memset(csum[:, 0:1], 0.0)
                comp("vector", em9, writes=[B_csum])

                if p == 2:
                    def emb1(e):
                        return e.tensor_scalar(csb[:, 0:SEG], csum[:, 1:SEG + 1],
                                               csum[:, SEG:SEG + 1], -1.0,
                                               ALU.subtract, ALU.mult)
                    comp("vector", emb1, reads=[B_csum], writes=[B_csb], fence=True)

                    def emb2(e):
                        return e.tensor_add(csb[:, 0:SEG], csb[:, 0:SEG], sp_a[:, :])
                    comp("vector", emb2, reads=[B_spa, B_csb], writes=[B_csb])

                    def emb3(e):
                        return e.memset(csb[:, SEG:SEG + 1], 0.0)
                    comp("vector", emb3, writes=[B_csb])

                CS = csum if p == 1 else csb
                B_CS = B_csum if p == 1 else B_csb
                voff = 1 if p == 1 else 0
                drain("vector", CS[0:32, SEG - 3:SEG + 1], B_CS)
                S.dma("sync", bass.AP(scd_dram[p].tensor, 0, [[SEG, 32], [1, SEG]]),
                      CS[:, voff:voff + SEG], reads=[B_CS], writes=[B_scd[p]])

                def em10(e, CS=None):
                    return e.tensor_scalar_mul(ncsum[:, :], CS[:, :], -1.0)
                def em10b(e, CS=CS):
                    return e.tensor_scalar_mul(ncsum[:, :], CS[:, :], -1.0)
                comp("vector", em10b, reads=[B_CS], writes=[B_ncsum])
                if f"ncsum{p}" in taps:
                    S.dma("sync", taps[f"ncsum{p}"], ncsum[:, :], reads=[B_ncsum])

                for k in range(NCHUNK):
                    if p == 1:
                        vc, entry, exit_ = 1 + 128 * k, 128 * k, 128 * (k + 1)
                    else:
                        vc, entry, exit_ = 128 * k, 128 * (k + 1), 128 * k

                    def em_w(e, vc=vc, exit_=exit_, CS=CS):
                        return e.activation(sc32[:, :], CS[:, vc:vc + 128], AF.Exp,
                                            scale=-1.0, bias=CS[:, exit_:exit_ + 1])
                    comp("scalar", em_w, reads=[B_CS], writes=[B_sc32])
                    drain("scalar", sc32[0:32, 124:128], B_sc32)
                    S.dma("sync",
                          bass.AP(scd_dram[p].tensor, 64 * SEG + 128 * k,
                                  [[SEG, 32], [1, 128]]),
                          sc32[:, :], reads=[B_sc32], writes=[B_scd[p]])

                    def em_u(e, vc=vc, entry=entry, CS=CS):
                        return e.activation(sc32[:, :], CS[:, vc:vc + 128], AF.Exp,
                                            bias=ncsum[:, entry:entry + 1])
                    comp("scalar", em_u, reads=[B_CS, B_ncsum], writes=[B_sc32])
                    drain("scalar", sc32[0:32, 124:128], B_sc32)
                    S.dma("sync",
                          bass.AP(u_dram[p].tensor, 128 * k, [[SEG, 32], [1, 128]]),
                          sc32[:, :], reads=[B_sc32], writes=[B_u[p]])

                ex0, en0 = (128, 0) if p == 1 else (0, 128)

                def em_d1(e, ex0=ex0, en0=en0, CS=CS):
                    ap_ex = CS[:, ex0::128][:, 0:32]
                    ap_en = CS[:, en0::128][:, 0:32]
                    return e.tensor_sub(dec32[:, :], ap_ex, ap_en)
                comp("vector", em_d1, reads=[B_CS], writes=[B_dec32], fence=True)

                def em_d2(e):
                    return e.activation(dec32[:, :], dec32[:, :], AF.Exp)
                comp("scalar", em_d2, reads=[B_dec32], writes=[B_dec32])
                drain("scalar", dec32[0:32, 28:32], B_dec32)
                S.dma("sync", bass.AP(dec_dram[p].tensor, 0, [[1, 32], [32, 32]]),
                      dec32[:, :], reads=[B_dec32], writes=[B_dec[p]],
                      allow_slow_non_contiguous=True)

                if f"csum{p}" in taps:
                    S.dma("sync", taps[f"csum{p}"], CS[:, :], reads=[B_CS])
                if f"udram{p}" in taps:
                    S.dma("sync", taps[f"udram{p}"], u_dram[p], reads=[B_u[p]])
                if f"scddram{p}" in taps:
                    S.dma("sync", taps[f"scddram{p}"], scd_dram[p], reads=[B_scd[p]])
            if "xn" in taps:
                S.dma("sync", taps["xn"], xn_dram, reads=[B_xn_dram])
            S.barrier(allbufs)

        with contextlib.ExitStack() as dst_:
            dent = dst_.enter_context
            DPL = 4224
            Win_bf = dent(nc.sbuf_tensor("Win_bf", [128, 8 * DPL], BF16))
            B_Win = mkbuf("Win_bf")
            Wout_bf = dent(nc.sbuf_tensor("Wout_bf", [128, 16 * D], BF16))
            B_Wout = mkbuf("Wout_bf")
            Wo_bf = dent(nc.sbuf_tensor("Wo_bf", [128, 16 * D], BF16))
            B_Wo = mkbuf("Wo_bf")
            xn_hs = [dent(nc.sbuf_tensor(f"xn_hs{i}", [128, 8 * HS], BF16))
                     for i in range(2)]
            B_xn_hs = [mkbuf(f"xn_hs{i}") for i in range(2)]
            zs_hs = dent(nc.sbuf_tensor("zs_hs", [128, 16 * HS], BF16))
            B_zs = mkbuf("zs_hs")
            xbc_xs = dent(nc.sbuf_tensor("xbc_xs", [128, 16 * (HS + 3)], BF16))
            B_xbc = mkbuf("xbc_xs")
            xbc_B = dent(nc.sbuf_tensor("xbc_B", [64, HS + 3], BF16))
            B_xbcB = mkbuf("xbc_B")
            xbc_C = dent(nc.sbuf_tensor("xbc_C", [64, HS + 3], BF16))
            B_xbcC = mkbuf("xbc_C")
            cxs = dent(nc.sbuf_tensor("cxs", [128, 16 * HS], BF16))
            B_cxs = mkbuf("cxs")
            convacc = dent(nc.sbuf_tensor("convacc", [128, HS], BF16))
            B_cacc = mkbuf("convacc")
            cB = dent(nc.sbuf_tensor("cB", [64, HS], BF16))
            B_cB = mkbuf("cB")
            cC = dent(nc.sbuf_tensor("cC", [64, HS], BF16))
            B_cC = mkbuf("cC")
            small_src = dent(nc.sbuf_tensor("small_src", [128, 128], F32))
            B_ssrc = mkbuf("small_src")
            small_sb = dent(nc.sbuf_tensor("small_sb", [128, 96], F32))
            B_ssb = mkbuf("small_sb")
            dtw_bf = dent(nc.sbuf_tensor("dtw_bf", [128, 64], BF16))
            B_dtw = mkbuf("dtw_bf")
            CT = dent(nc.sbuf_tensor("CT", [128, 512], F32))
            B_CT = mkbuf("CT")
            E_sb = dent(nc.sbuf_tensor("E_sb", [128, 512], BF16))
            B_E = mkbuf("E_sb")
            S_sb = E_sb
            B_S = B_E
            G_sb = dent(nc.sbuf_tensor("G_sb", [128, 128], BF16))
            B_G = mkbuf("G_sb")
            Bstage = dent(nc.sbuf_tensor("Bstage", [128, 128], BF16))
            B_Bst = mkbuf("Bstage")
            B_tm = dent(nc.sbuf_tensor("B_tm", [128, 64], BF16))
            B_Btm = mkbuf("B_tm")
            xs_tm = dent(nc.sbuf_tensor("xs_tm", [128, 2048], BF16))
            B_xstm = mkbuf("xs_tm")
            Xt = xs_tm
            B_Xt = B_xstm
            Xh = dent(nc.sbuf_tensor("Xh", [128, 2048], BF16))
            B_Xh = mkbuf("Xh")
            u64 = dent(nc.sbuf_tensor("u64", [64, 512], BF16))
            B_u64 = mkbuf("u64")
            dec64 = dent(nc.sbuf_tensor("dec64", [64, 32], F32))
            B_dec64 = mkbuf("dec64")
            Chat = dent(nc.sbuf_tensor("Chat", [64, 512], BF16))
            B_Chat = mkbuf("Chat")
            dxs = dent(nc.sbuf_tensor("dxs", [128, 512], BF16))
            B_dxs = mkbuf("dxs")
            y1 = dent(nc.sbuf_tensor("y1", [128, 512], F32))
            B_y1 = mkbuf("y1")
            ySq = dxs
            B_ySq = B_dxs
            yg_hs = xbc_xs[:, 0:16 * HS]
            B_yg = B_xbc
            xtail = dent(nc.sbuf_tensor("xtail", [128, 48], BF16))
            B_xtail = mkbuf("xtail")
            btail = dent(nc.sbuf_tensor("btail", [64, 6], BF16))
            B_btail = mkbuf("btail")
            rstd_c = dent(nc.sbuf_tensor("rstd_c", [1, HS], F32))
            B_rstd = mkbuf("rstd_c")
            rstd_b = dent(nc.sbuf_tensor("rstd_b", [1, HS], BF16))
            B_rstdb = mkbuf("rstd_b")
            rstdx = dent(nc.sbuf_tensor("rstdx", [128, HS], BF16))
            B_rstdx = mkbuf("rstdx")
            P_sb = dent(nc.sbuf_tensor("P_sb", [128, 8 * HS], BF16))
            B_P = mkbuf("P_sb")
            P1_sb = dent(nc.sbuf_tensor("P1_sb", [128, 8 * HS], BF16))
            B_P1 = mkbuf("P1_sb")
            s_sb = P1_sb
            B_s = B_P1

            for dpass in (1, 2):
                rev = (dpass == 2)
                p = dpass
                for k in range(8):
                    S.dma("gpsimd", Win_bf[:, DPL * k:DPL * (k + 1)],
                          Win_d[p][128 * k:128 * (k + 1), 0:DPL], writes=[B_Win])
                for k in range(16):
                    S.dma("gpsimd", Wout_bf[:, D * k:D * (k + 1)],
                          Wout_d[p][128 * k:128 * (k + 1), :], writes=[B_Wout])
                for k in range(16):
                    def em_f(e, k=k, p=p):
                        return e.tensor_scalar_mul(Wout_bf[:, D * k:D * (k + 1)],
                                                   Wout_bf[:, D * k:D * (k + 1)],
                                                   normw[p][:, k:k + 1])
                    comp("vector", em_f, reads=[B_Wout, B_consts], writes=[B_Wout])
                if dpass == 2:
                    for k in range(8):
                        S.dma("gpsimd", Wo_bf[:, D * k:D * (k + 1)],
                              Wo_d[1][128 * k:128 * (k + 1), :], writes=[B_Wo])
                    for k in range(8):
                        S.dma("gpsimd", Wo_bf[:, D * (8 + k):D * (9 + k)],
                              Wo_d[2][128 * k:128 * (k + 1), :], writes=[B_Wo])

                if dpass == 1:
                    comp("vector", lambda e: e.memset(states_f[:, :], 0.0),
                         writes=[B_stf])
                else:
                    drain("vector", states_f[0:64, 2044:2048], B_stf)
                    S.dma("sync", coll_in, states_f[:, :], reads=[B_stf],
                          writes=[B_coll])

                    def em_coll(e):
                        return e.collective_compute(
                            "AllReduce", ALU.add,
                            replica_groups=[[0, 1], [2, 3], [4, 5], [6, 7]],
                            ins=[coll_in.opt()], outs=[coll_out.opt()])
                    S.add("gpsimd", em_coll, reads=[B_coll], writes=[B_coll],
                          kind="coll")
                    def em_neg(e):
                        return e.tensor_scalar_mul(states_f[:, :], states_f[:, :], -1.0)
                    comp("vector", em_neg, reads=[B_stf], writes=[B_stf])
                    S.dma("gpsimd", states_f[:, :], coll_out, accum_op=ALU.add,
                          reads=[B_coll], writes=[B_stf])

                def em_cb0(e):
                    return e.tensor_copy(states_b[:, :], states_f[:, :])
                comp("vector", em_cb0, reads=[B_stf], writes=[B_stb])

                halo_col = 0 if not rev else SEG + 128
                halo_take = slice(125, 128) if not rev else slice(0, 3)
                tail_dst = 0 if not rev else HS
                hx, bhx = xn_hs[0], B_xn_hs[0]
                src = bass.AP(xn_dram.tensor, halo_col,
                              [[8 * XN_STR, 128], [XN_STR, 8], [1, 128]])
                S.dma("sync", hx[:, 0:1024].rearrange("p (k t) -> p k t", k=8), src,
                      reads=[B_xn_dram], writes=[bhx])
                for j in range(16):
                    jj = 16 + j
                    for k in range(8):
                        def em_mm(e, jj=jj, k=k, hx=hx):
                            return e.matmul(
                                ps_proj[:, 0:128],
                                Win_bf[:, DPL * k + 128 * jj:DPL * k + 128 * jj + 128],
                                hx[:, 128 * k:128 * k + 128],
                                start=(k == 0), stop=(k == 7))
                        comp("tensor", em_mm, reads=[B_Win, bhx], writes=[B_proj[0]])

                    def em_ev(e, j=j, tail_dst=tail_dst, halo_take=halo_take):
                        dst = xbc_xs[:, (HS + 3) * j + tail_dst:(HS + 3) * j + tail_dst + 3]
                        if HALO_ZERO_HACK:
                            return e.memset(dst, 0.0)
                        return e.tensor_copy(dst, ps_proj[:, halo_take])
                    comp("vector", em_ev, reads=[B_proj[0]], writes=[B_xbc])
                for (colbase, dstb, bb) in ((4096, xbc_B, B_xbcB), (4160, xbc_C, B_xbcC)):
                    for k in range(8):
                        def em_mm(e, colbase=colbase, k=k, hx=hx):
                            return e.matmul(
                                ps_proj[0:64, 0:128],
                                Win_bf[:, DPL * k + colbase:DPL * k + colbase + 64],
                                hx[:, 128 * k:128 * k + 128],
                                start=(k == 0), stop=(k == 7))
                        comp("tensor", em_mm, reads=[B_Win, bhx], writes=[B_proj[0]])

                    def em_ev(e, dstb=dstb, bb=bb, tail_dst=tail_dst,
                              halo_take=halo_take):
                        if HALO_ZERO_HACK:
                            return e.memset(dstb[:, tail_dst:tail_dst + 3], 0.0)
                        return e.tensor_copy(dstb[:, tail_dst:tail_dst + 3],
                                             ps_proj[0:64, halo_take])
                    comp("vector", em_ev, reads=[B_proj[0]], writes=[bb])

                nhs = NHS if HS_LIMIT is None else HS_LIMIT
                hs_order = list(range(nhs)) if not rev else list(range(NHS - 1,
                                                                       NHS - 1 - nhs, -1))
                first = True
                raw_off = 3 if not rev else 0
                for hi, hs_i in enumerate(hs_order):
                    t0 = HS * hs_i
                    par = hi % 2
                    hx, bhx = xn_hs[par], B_xn_hs[par]
                    src = bass.AP(xn_dram.tensor, 128 + t0,
                                  [[8 * XN_STR, 128], [XN_STR, 8], [1, HS]])
                    S.dma("sync", hx[:, :].rearrange("p (k t) -> p k t", k=8), src,
                          reads=[B_xn_dram], writes=[bhx])

                    tail_src = slice(HS, HS + 3) if not rev else slice(0, 3)
                    tail_slot = slice(0, 3) if not rev else slice(HS, HS + 3)
                    if not first:
                        def em_t1(e, tail_slot=tail_slot):
                            v = xbc_xs[:, :].rearrange("p (j c) -> p j c", j=16)
                            return e.tensor_copy(
                                v[:, :, tail_slot],
                                xtail[:, :].rearrange("p (j c) -> p j c", j=16))
                        comp("vector", em_t1, reads=[B_xtail], writes=[B_xbc])

                        def em_t2(e, tail_slot=tail_slot):
                            return e.tensor_copy(xbc_B[:, tail_slot], btail[:, 0:3])
                        comp("vector", em_t2, reads=[B_btail], writes=[B_xbcB])

                        def em_t3(e, tail_slot=tail_slot):
                            return e.tensor_copy(xbc_C[:, tail_slot], btail[:, 3:6])
                        comp("vector", em_t3, reads=[B_btail], writes=[B_xbcC])
                    first = False

                    for j in range(33):
                        pp = ps_proj if (j % 2 == 0) else ps_proj2
                        bpp = B_proj[j % 2]
                        if j < 32:
                            m0, mwid = 128 * j, 128
                        else:
                            m0, mwid = 4096, 64
                        for k in range(8):
                            def em_mm(e, k=k, m0=m0, mwid=mwid, pp=pp, hx=hx):
                                return e.matmul(
                                    pp[0:mwid, 0:HS],
                                    Win_bf[:, DPL * k + m0:DPL * k + m0 + mwid],
                                    hx[:, HS * k:HS * k + HS],
                                    start=(k == 0), stop=(k == 7))
                            comp("tensor", em_mm, reads=[B_Win, bhx], writes=[bpp])
                        if j == 32:
                            for k in range(8):
                                def em_mm(e, k=k, hx=hx):
                                    return e.matmul(
                                        ps_proj2[0:64, 0:HS],
                                        Win_bf[:, DPL * k + 4160:DPL * k + 4160 + 64],
                                        hx[:, HS * k:HS * k + HS],
                                        start=(k == 0), stop=(k == 7))
                                comp("tensor", em_mm, reads=[B_Win, bhx],
                                     writes=[B_proj[1]])

                            def em_evB(e, raw_off=raw_off):
                                return e.tensor_copy(xbc_B[:, raw_off:raw_off + HS],
                                                     ps_proj[0:64, 0:HS])
                            comp("vector", em_evB, reads=[B_proj[0]], writes=[B_xbcB])

                            def em_evC(e, raw_off=raw_off):
                                return e.tensor_copy(xbc_C[:, raw_off:raw_off + HS],
                                                     ps_proj2[0:64, 0:HS])
                            comp("vector", em_evC, reads=[B_proj[1]], writes=[B_xbcC])
                        elif j < 16:
                            def em_ev(e, j=j, pp=pp):
                                return e.activation(zs_hs[:, HS * j:HS * (j + 1)],
                                                    pp[:, 0:HS], AF.Silu)
                            comp("scalar", em_ev, reads=[bpp], writes=[B_zs])
                        else:
                            jx = j - 16

                            def em_ev(e, jx=jx, pp=pp, raw_off=raw_off):
                                return e.tensor_copy(
                                    xbc_xs[:, (HS + 3) * jx + raw_off:
                                           (HS + 3) * jx + raw_off + HS],
                                    pp[:, 0:HS])
                            comp("vector", em_ev, reads=[bpp], writes=[B_xbc])

                    for j in range(16):
                        base = (HS + 3) * j
                        for k in range(4):
                            off = k if not rev else 3 - k

                            def em_c(e, j=j, k=k, off=off, base=base, p=p):
                                win = xbc_xs[:, base + off:base + off + HS]
                                wcol = wxs[p][:, 4 * j + k:4 * j + k + 1]
                                if k == 0:
                                    return e.tensor_scalar_mul(convacc[:, :], win, wcol)
                                return e.scalar_tensor_tensor(
                                    convacc[:, :], win, wcol, convacc[:, :],
                                    ALU.mult, ALU.add)
                            rds = [B_xbc, B_consts] + ([B_cacc] if k else [])
                            comp("vector", em_c, reads=rds, writes=[B_cacc])

                        def em_s(e, j=j, p=p):
                            return e.activation(cxs[:, HS * j:HS * (j + 1)],
                                                convacc[:, :], AF.Silu,
                                                bias=bxs[p][:, j:j + 1])
                        comp("scalar", em_s, reads=[B_cacc, B_consts], writes=[B_cxs])
                    for (srcb, bs, dstb, bd, wofs, bofs) in (
                            (xbc_B, B_xbcB, cB, B_cB, 0, 0),
                            (xbc_C, B_xbcC, cC, B_cC, 4, 1)):
                        for k in range(4):
                            off = k if not rev else 3 - k

                            def em_c(e, srcb=srcb, k=k, off=off, wofs=wofs, p=p):
                                win = srcb[:, off:off + HS]
                                wcol = wBC[p][:, wofs + k:wofs + k + 1]
                                if k == 0:
                                    return e.tensor_scalar_mul(convacc[0:64, :], win,
                                                               wcol)
                                return e.scalar_tensor_tensor(
                                    convacc[0:64, :], win, wcol, convacc[0:64, :],
                                    ALU.mult, ALU.add)
                            rds = [bs, B_consts] + ([B_cacc] if k else [])
                            comp("vector", em_c, reads=rds, writes=[B_cacc])

                        def em_s(e, dstb=dstb, bofs=bofs, p=p):
                            return e.activation(dstb[:, :], convacc[0:64, :], AF.Silu,
                                                bias=bBC[p][:, bofs:bofs + 1])
                        comp("scalar", em_s, reads=[B_cacc, B_consts], writes=[bd])

                    def em_sp1(e, tail_src=tail_src):
                        v = xbc_xs[:, :].rearrange("p (j c) -> p j c", j=16)
                        return e.tensor_copy(
                            xtail[:, :].rearrange("p (j c) -> p j c", j=16),
                            v[:, :, tail_src])
                    comp("vector", em_sp1, reads=[B_xbc], writes=[B_xtail])

                    def em_sp2(e, tail_src=tail_src):
                        return e.tensor_copy(btail[:, 0:3], xbc_B[:, tail_src])
                    comp("vector", em_sp2, reads=[B_xbcB], writes=[B_btail])

                    def em_sp3(e, tail_src=tail_src):
                        return e.tensor_copy(btail[:, 3:6], xbc_C[:, tail_src])
                    comp("vector", em_sp3, reads=[B_xbcC], writes=[B_btail])

                    if dpass == TAP_PASS and hs_i == TAP_HS:
                        tap("cxs", cxs[:, :], B_cxs)
                        tap("zs", zs_hs[:, :], B_zs)
                        tap("cB", cB[:, :], B_cB)
                        tap("cC", cC[:, :], B_cC)

                    ch_order = (0, 1) if not rev else (1, 0)
                    for c in ch_order:
                        kg = 2 * hs_i + c
                        tg = 128 * kg
                        cin = 128 * c

                        src = bass.AP(scd_dram[p].tensor, tg, [[SEG, 96], [1, 128]])
                        S.dma("sync", small_src[0:96, :], src, reads=[B_scd[p]],
                              writes=[B_ssrc])

                        def em_st(e):
                            return e.transpose(ps_gs[:, :], small_src[:, :], id_f[:, :])
                        comp("tensor", em_st, reads=[B_ssrc, B_consts], writes=[B_gs])

                        def em_se(e):
                            return e.tensor_copy(small_sb[:, :], ps_gs[:, 0:96])
                        comp("vector", em_se, reads=[B_gs], writes=[B_ssb])

                        def em_sc(e):
                            return e.tensor_copy(dtw_bf[:, :], small_sb[:, 32:96])
                        comp("vector", em_sc, reads=[B_ssb], writes=[B_dtw])
                        if dpass == TAP_PASS and hs_i == TAP_HS and c == TAP_CHUNK:
                            tap("smallsb", small_sb[:, :], B_ssb)

                        src = bass.AP(dec_dram[p].tensor, 32 * kg, [[0, 64], [1, 32]])
                        S.dma("sync", dec64[:, :], src, reads=[B_dec[p]],
                              writes=[B_dec64])

                        def em_g(e, cin=cin):
                            return e.matmul(ps_gs[:, :], cB[:, cin:cin + 128],
                                            cC[:, cin:cin + 128], start=True, stop=True)
                        comp("tensor", em_g, reads=[B_cB, B_cC, B_gs], writes=[B_gs])

                        def em_ge(e):
                            return e.tensor_copy(G_sb[:, :], ps_gs[:, :])
                        comp("vector", em_ge, reads=[B_gs], writes=[B_G])
                        if dpass == TAP_PASS and hs_i == TAP_HS and c == TAP_CHUNK:
                            tap("Gsb", G_sb[:, :], B_G)

                        def em_b0(e, cin=cin):
                            return e.tensor_copy(Bstage[0:64, :], cB[:, cin:cin + 128])
                        comp("vector", em_b0, reads=[B_cB], writes=[B_Bst])

                        def em_bt(e):
                            return e.transpose(ps_t[:, 0:128], Bstage[:, :], id_b[:, :])
                        comp("tensor", em_bt, reads=[B_Bst, B_consts], writes=[B_t])

                        def em_be(e):
                            return e.tensor_copy(B_tm[:, :], ps_t[:, 0:64])
                        comp("vector", em_be, reads=[B_t], writes=[B_Btm])
                        if dpass == TAP_PASS and hs_i == TAP_HS and c == TAP_CHUNK:
                            tap("Btm", B_tm[:, :], B_Btm)

                        for g in range(4):
                            for kk in range(4):
                                j = 4 * g + kk

                                def em_tp(e, j=j, kk=kk, cin=cin):
                                    return e.transpose(
                                        ps_t[:, 128 * kk:128 * kk + 128],
                                        cxs[:, HS * j + cin:HS * j + cin + 128],
                                        id_b[:, :])
                                comp("tensor", em_tp, reads=[B_cxs, B_consts],
                                     writes=[B_t])

                            def em_ev(e, g=g):
                                return e.tensor_copy(xs_tm[:, 512 * g:512 * g + 512],
                                                     ps_t[:, :])
                            comp("vector", em_ev, reads=[B_t], writes=[B_xstm])

                        if dpass == TAP_PASS and hs_i == TAP_HS and c == TAP_CHUNK:
                            tap("xstm", xs_tm[:, :], B_xstm)

                        def em_xt(e):
                            dt_b = dtw_bf[:, 0:32].unsqueeze(2).broadcast_to(
                                [128, 32, 64])
                            return e.tensor_tensor(
                                Xt[:, :].rearrange("p (h q) -> p h q", h=32),
                                xs_tm[:, :].rearrange("p (h q) -> p h q", h=32),
                                dt_b, ALU.mult)
                        comp("vector", em_xt, reads=[B_xstm, B_dtw], writes=[B_Xt])

                        def em_xh(e):
                            w_b = dtw_bf[:, 32:64].unsqueeze(2).broadcast_to(
                                [128, 32, 64])
                            return e.tensor_tensor(
                                Xh[:, :].rearrange("p (h q) -> p h q", h=32),
                                Xt[:, :].rearrange("p (h q) -> p h q", h=32),
                                w_b, ALU.mult)
                        comp("vector", em_xh, reads=[B_Xt, B_dtw], writes=[B_Xh])
                        if dpass == TAP_PASS and hs_i == TAP_HS and c == TAP_CHUNK:
                            tap("Xtd", Xt[:, :], B_Xt)
                            tap("Xhd", Xh[:, :], B_Xh)

                        for q in range(8):
                            mb_src = bass.AP(MB_d[p].tensor, 0,
                                             [[128, 128], [0, 4], [1, 128]])
                            S.dma("sync", CT[:, :].rearrange("p (h t) -> p h t", h=4),
                                  mb_src, reads=[B_consts], writes=[B_CT])
                            cs_src = bass.AP(scd_dram[p].tensor, 4 * q * SEG + tg,
                                             [[0, 128], [SEG, 4], [1, 128]])
                            S.dma("gpsimd", CT[:, :].rearrange("p (h t) -> p h t", h=4),
                                  cs_src, accum_op=ALU.add, reads=[B_scd[p]],
                                  writes=[B_CT])

                            def em_m(e, q=q):
                                cs_b = small_sb[:, 4 * q:4 * q + 4].unsqueeze(2) \
                                    .broadcast_to([128, 4, 128])
                                v = CT[:, :].rearrange("p (h t) -> p h t", h=4)
                                return e.tensor_tensor(v, v, cs_b, ALU.subtract)
                            comp("vector", em_m, reads=[B_CT, B_ssb], writes=[B_CT])
                            if dpass == TAP_PASS and hs_i == TAP_HS and c == TAP_CHUNK and q == TAP_Q:
                                tap("CTd", CT[:, :], B_CT)

                            def em_e(e):
                                return e.activation(E_sb[:, :], CT[:, :], AF.Exp)
                            comp("scalar", em_e, reads=[B_CT], writes=[B_E])

                            def em_sq(e):
                                g_b = G_sb[:, :].unsqueeze(1).broadcast_to([128, 4, 128])
                                return e.tensor_tensor(
                                    S_sb[:, :].rearrange("p (h t) -> p h t", h=4),
                                    E_sb[:, :].rearrange("p (h t) -> p h t", h=4),
                                    g_b, ALU.mult)
                            comp("vector", em_sq, reads=[B_E, B_G], writes=[B_S])
                            if dpass == TAP_PASS and hs_i == TAP_HS and c == TAP_CHUNK and q == TAP_Q:
                                tap("Ssb", S_sb[:, :], B_S)

                            u_src = bass.AP(u_dram[p].tensor, 4 * q * SEG + tg,
                                            [[0, 64], [SEG, 4], [1, 128]])
                            S.dma("gpsimd", u64[:, :].rearrange("p (h t) -> p h t", h=4),
                                  u_src, reads=[B_u[p]], writes=[B_u64])

                            def em_ch(e, cin=cin):
                                c_b = cC[:, cin:cin + 128].unsqueeze(1).broadcast_to(
                                    [64, 4, 128])
                                return e.tensor_tensor(
                                    Chat[:, :].rearrange("p (h t) -> p h t", h=4),
                                    c_b,
                                    u64[:, :].rearrange("p (h t) -> p h t", h=4),
                                    ALU.mult)
                            comp("vector", em_ch, reads=[B_cC, B_u64], writes=[B_Chat])
                            if dpass == TAP_PASS and hs_i == TAP_HS and c == TAP_CHUNK and q == TAP_Q:
                                tap("u64d", u64[:, :], B_u64)
                                tap("Chatd", Chat[:, :], B_Chat)

                            for hh in range(4):
                                h = 4 * q + hh
                                j = h // 2
                                parq = h % 2
                                bank = B_main[j // 4]
                                yv = ps_main[64 * parq:64 * parq + 64,
                                             128 * j:128 * j + 128]

                                def em_yi(e, h=h, hh=hh, yv=yv, parq=parq):
                                    return e.matmul(yv, Xt[:, 64 * h:64 * h + 64],
                                                    S_sb[:, 128 * hh:128 * hh + 128],
                                                    start=True, stop=False,
                                                    tile_position=(0, 64 * parq))
                                comp("tensor", em_yi, reads=[B_Xt, B_S], writes=[bank])

                                def em_ye(e, h=h, hh=hh, yv=yv, parq=parq):
                                    return e.matmul(yv,
                                                    states_b[:, 64 * h:64 * h + 64],
                                                    Chat[:, 128 * hh:128 * hh + 128],
                                                    start=False, stop=True,
                                                    tile_position=(0, 64 * parq))
                                comp("tensor", em_ye, reads=[B_stb, B_Chat],
                                     writes=[bank])

                        for q in range(4):
                            def em_cs(e, q=q):
                                return e.matmul(ps_aux[0:64, :], B_tm[:, :],
                                                Xh[:, 512 * q:512 * q + 512],
                                                start=True, stop=True)
                            comp("tensor", em_cs, reads=[B_Btm, B_Xh, B_aux],
                                 writes=[B_aux])

                            def em_u1(e, q=q):
                                d_b = dec64[:, 8 * q:8 * q + 8].unsqueeze(2) \
                                    .broadcast_to([64, 8, 64])
                                sl = states_f[:, 512 * q:512 * q + 512].rearrange(
                                    "p (h v) -> p h v", h=8)
                                return e.tensor_tensor(sl, sl, d_b, ALU.mult)
                            comp("vector", em_u1, reads=[B_stf, B_dec64],
                                 writes=[B_stf])

                            def em_u2(e, q=q):
                                sl = states_f[:, 512 * q:512 * q + 512]
                                return e.tensor_add(sl, sl, ps_aux[0:64, :])
                            comp("vector", em_u2, reads=[B_stf, B_aux], writes=[B_stf])

                        def em_cb(e):
                            return e.tensor_copy(states_b[:, :], states_f[:, :])
                        comp("vector", em_cb, reads=[B_stf], writes=[B_stb])
                        if dpass == TAP_PASS and hs_i == TAP_HS and c == TAP_CHUNK:
                            tap("states0", states_f[:, :], B_stf)
                            tap("dec64d", dec64[:, :], B_dec64)

                        for q in range(4):
                            bank = B_main[q]
                            yq = ps_main[:, 512 * q:512 * q + 512]

                            def em_dx(e, q=q, cin=cin, p=p):
                                xsf = cxs[:, :].rearrange("p (j c) -> p j c", j=16)[
                                    :, 4 * q:4 * q + 4, cin:cin + 128]
                                d_b = D16[p][:, 4 * q:4 * q + 4].unsqueeze(2) \
                                    .broadcast_to([128, 4, 128])
                                return e.tensor_tensor(
                                    dxs[:, :].rearrange("p (j c) -> p j c", j=4),
                                    xsf, d_b, ALU.mult)
                            comp("vector", em_dx, reads=[B_cxs, B_consts],
                                 writes=[B_dxs])

                            def em_y1(e, yq=yq):
                                return e.tensor_add(y1[:, :], dxs[:, :], yq)
                            comp("vector", em_y1, reads=[B_dxs, bank], writes=[B_y1])
                            if dpass == TAP_PASS and hs_i == TAP_HS and c == TAP_CHUNK and q == TAP_Q:
                                tap("y1d", y1[:, :], B_y1)
                                tap("dxsd", dxs[:, :], B_dxs)

                            def em_yg(e, q=q, cin=cin):
                                zf = zs_hs[:, :].rearrange("p (j c) -> p j c", j=16)[
                                    :, 4 * q:4 * q + 4, cin:cin + 128]
                                ygv = yg_hs[:, :].rearrange("p (j c) -> p j c", j=16)[
                                    :, 4 * q:4 * q + 4, cin:cin + 128]
                                return e.tensor_tensor(
                                    ygv, y1[:, :].rearrange("p (j c) -> p j c", j=4),
                                    zf, ALU.mult)
                            comp("vector", em_yg, reads=[B_y1, B_zs], writes=[B_yg])

                            def em_sq2(e, q=q, cin=cin):
                                ygv = yg_hs[:, :].rearrange("p (j c) -> p j c", j=16)[
                                    :, 4 * q:4 * q + 4, cin:cin + 128]
                                return e.activation(
                                    ySq[:, :].rearrange("p (j c) -> p j c", j=4),
                                    ygv, AF.Square)
                            comp("scalar", em_sq2, reads=[B_yg], writes=[B_ySq])

                            for jj in range(4):
                                def em_rm(e, jj=jj, q=q):
                                    return e.matmul(ps_aux[64:65, 0:128],
                                                    ones_col[:, :],
                                                    ySq[:, 128 * jj:128 * jj + 128],
                                                    start=(q == 0 and jj == 0),
                                                    stop=(q == 3 and jj == 3),
                                                    tile_position=(0, 64))
                                comp("tensor", em_rm, reads=[B_ySq, B_consts, B_aux],
                                     writes=[B_aux])

                        def em_rs(e, cin=cin):
                            return e.activation(rstd_c[0:1, cin:cin + 128],
                                                ps_aux[64:65, 0:128], AF.Sqrt,
                                                scale=1.0 / DI, bias=1e-5)
                        comp("scalar", em_rs, reads=[B_aux], writes=[B_rstd])

                        def em_rr(e, cin=cin):
                            return e.reciprocal(rstd_c[0:1, cin:cin + 128],
                                                rstd_c[0:1, cin:cin + 128])
                        comp("vector", em_rr, reads=[B_rstd], writes=[B_rstd])

                    if dpass == TAP_PASS and hs_i == TAP_HS:
                        tap("yg", yg_hs[:, :], B_yg)
                        tap("rstdc", rstd_c[:, :], B_rstd)

                    def em_rb(e):
                        return e.tensor_copy(rstd_b[:, :], rstd_c[:, :])
                    comp("vector", em_rb, reads=[B_rstd], writes=[B_rstdb], fence=True)

                    def em_rx(e):
                        return e.matmul(ps_aux[:, 0:HS], ones_row[:, :], rstd_b[:, :],
                                        start=True, stop=True)
                    comp("tensor", em_rx, reads=[B_rstdb, B_aux], writes=[B_aux])

                    def em_rxe(e):
                        return e.tensor_copy(rstdx[:, :], ps_aux[:, 0:HS])
                    comp("vector", em_rxe, reads=[B_aux], writes=[B_rstdx])

                    for m in range(8):
                        for k in range(16):
                            def em_mm(e, m=m, k=k):
                                return e.matmul(
                                    ps_main[:, HS * m:HS * (m + 1)],
                                    Wout_bf[:, D * k + 128 * m:D * k + 128 * m + 128],
                                    yg_hs[:, HS * k:HS * (k + 1)],
                                    start=(k == 0), stop=(k == 15))
                            comp("tensor", em_mm,
                                 reads=[B_Wout, B_yg, B_main[m // 2]],
                                 writes=[B_main[m // 2]])

                    def em_pe(e):
                        r_b = rstdx[:, :].unsqueeze(1).broadcast_to([128, 8, HS])
                        return e.tensor_tensor(
                            P_sb[:, :].rearrange("p (m t) -> p m t", m=8),
                            ps_main[:, 0:8 * HS].rearrange("p (m t) -> p m t", m=8),
                            r_b, ALU.mult)
                    comp("vector", em_pe, reads=[B_rstdx] + B_main, writes=[B_P])
                    drain("vector", P_sb[:, 8 * HS - 4:8 * HS], B_P)

                    if dpass == 1:
                        dstp = bass.AP(P_dram.tensor, t0,
                                       [[8 * SEG, 128], [SEG, 8], [1, HS]])
                        S.dma("sync", dstp,
                              P_sb[:, :].rearrange("p (m t) -> p m t", m=8),
                              reads=[B_P], writes=[B_P_dram])
                    else:
                        srcp = bass.AP(P_dram.tensor, t0,
                                       [[8 * SEG, 128], [SEG, 8], [1, HS]])
                        S.dma("sync",
                              P1_sb[:, :].rearrange("p (m t) -> p m t", m=8), srcp,
                              reads=[B_P_dram], writes=[B_P1])
                        for m in range(8):
                            for k in range(16):
                                kk = k % 8
                                rhsb, rb = (P1_sb, B_P1) if k < 8 else (P_sb, B_P)

                                def em_mm(e, m=m, k=k, kk=kk, rhsb=rhsb):
                                    return e.matmul(
                                        ps_main[:, HS * m:HS * (m + 1)],
                                        Wo_bf[:, D * k + 128 * m:D * k + 128 * m + 128],
                                        rhsb[:, HS * kk:HS * (kk + 1)],
                                        start=(k == 0), stop=(k == 15))
                                comp("tensor", em_mm,
                                     reads=[B_Wo, rb, B_main[m // 2]],
                                     writes=[B_main[m // 2]])
                        for m in range(8):
                            def em_sb(e, m=m):
                                return e.add(s_sb[:, HS * m:HS * (m + 1)],
                                             ps_main[:, HS * m:HS * (m + 1)],
                                             boc_sb[:, m:m + 1])
                            comp("scalar", em_sb, reads=[B_main[m // 2], B_consts],
                                 writes=[B_s])
                        drain("scalar", s_sb[:, 8 * HS - 4:8 * HS], B_s)
                        dsts = bass.AP(s_dram.tensor, t0,
                                       [[8 * SEG, 128], [SEG, 8], [1, HS]])
                        S.dma("sync", dsts,
                              s_sb[:, :].rearrange("p (m t) -> p m t", m=8),
                              reads=[B_s], writes=[B_s_dram])

                if dpass == 1 and "P1" in taps:
                    S.dma("sync", taps["P1"], P_dram, reads=[B_P_dram])
            if "s" in taps:
                S.dma("sync", taps["s"], s_dram, reads=[B_s_dram])
            S.barrier(allbufs)

        with contextlib.ExitStack() as fst:
            fent = fst.enter_context
            w1s = fent(nc.sbuf_tensor("w1s", [128, 8 * FH], BF16))
            B_w1s = mkbuf("w1s")
            w2s = fent(nc.sbuf_tensor("w2s", [128, 32 * D], BF16))
            B_w2s = mkbuf("w2s")
            wstage = fent(nc.sbuf_tensor("wstage", [128, FH], BF16))
            B_wst = mkbuf("wstage")
            arow_st = fent(nc.sbuf_tensor("arow_st", [1, 512], F32))
            B_arow = mkbuf("arow_st")
            abanks = [ps_main[0:1, 512 * jj:512 * jj + 512] for jj in range(4)] + \
                     [ps_proj[0:1, :], ps_proj2[0:1, :], ps_aux[0:1, :],
                      ps_tg[0:1, 0:512]]
            B_abanks = [B_main[0], B_main[1], B_main[2], B_main[3],
                        B_proj[0], B_proj[1], B_aux, B_t]
            a1c = fent(nc.sbuf_tensor("a1c", [128, 32], F32))
            B_a1c = mkbuf("a1c")
            a2c = fent(nc.sbuf_tensor("a2c", [128, 8], F32))
            B_a2c = mkbuf("a2c")
            b1c = fent(nc.sbuf_tensor("b1c_sb", [128, 32], F32))
            b2c = fent(nc.sbuf_tensor("b2c_sb", [128, 8], F32))
            B_bc = mkbuf("bc")
            xn_f = [fent(nc.sbuf_tensor(f"xn_f{i}", [128, 8 * HS], BF16))
                    for i in range(2)]
            B_xn_f = [mkbuf(f"xn_f{i}") for i in range(2)]
            h1g = fent(nc.sbuf_tensor("h1g", [128, 32 * HS], BF16))
            B_h1g = mkbuf("h1g")
            g_bf = fent(nc.sbuf_tensor("g_bf", [128, 8 * HS], BF16))
            B_g = mkbuf("g_bf")
            s_f = fent(nc.sbuf_tensor("s_f", [128, 8 * HS], BF16))
            B_sf = mkbuf("s_f")
            sg = fent(nc.sbuf_tensor("sg", [128, 8 * HS], BF16))
            B_sg = mkbuf("sg")
            x_f = [fent(nc.sbuf_tensor(f"x_f{i}", [128, D], F32)) for i in range(2)]
            B_x_f = [mkbuf(f"x_f{i}") for i in range(2)]
            o_f = fent(nc.sbuf_tensor("o_f", [128, D], F32))
            B_of = mkbuf("o_f")

            S.dma("sync", b1c[:, :], b1c_d, writes=[B_bc])
            S.dma("sync", b2c[:, :], b2c_d, writes=[B_bc])

            for k in range(8):
                S.dma("gpsimd", wstage[:, :], w1t_d[128 * k:128 * (k + 1), :],
                      writes=[B_wst])

                def em_ab(e, k=k):
                    return e.activation(w1s[:, FH * k:FH * (k + 1)], wstage[:, :],
                                        AF.Abs)
                comp("scalar", em_ab, reads=[B_wst], writes=[B_w1s])
                for nslc in range(8):
                    def em_mm(e, nslc=nslc, k=k):
                        return e.matmul(abanks[nslc], ones_col[:, :],
                                        w1s[:, FH * k + 512 * nslc:
                                            FH * k + 512 * (nslc + 1)],
                                        start=(k == 0), stop=(k == 7))
                    comp("tensor", em_mm, reads=[B_w1s, B_consts, B_abanks[nslc]],
                         writes=[B_abanks[nslc]])

                def em_sg_(e, k=k):
                    return e.sign(w1s[:, FH * k:FH * (k + 1)], wstage[:, :])
                comp("scalar", em_sg_, reads=[B_wst, B_w1s], writes=[B_w1s])
            for nslc in range(8):
                def em_ae(e, nslc=nslc):
                    return e.mul(arow_st[:, :], abanks[nslc], 1.0 / D)
                comp("scalar", em_ae, reads=[B_abanks[nslc]], writes=[B_arow])
                drain("scalar", arow_st[0:1, 508:512], B_arow)
                S.dma("sync",
                      bass.AP(alpha_dram.tensor, 512 * nslc, [[1, 1], [1, 512]]),
                      arow_st[:, :], reads=[B_arow], writes=[B_alpha_d])
            a_src = bass.AP(alpha_dram.tensor, 0, [[1, 128], [128, 32]])
            S.dma("sync", a1c[:, :], a_src, reads=[B_alpha_d], writes=[B_a1c],
                  allow_slow_non_contiguous=True)

            for k in range(32):
                S.dma("gpsimd", wstage[:, 0:D], w2t_d[128 * k:128 * (k + 1), :],
                      writes=[B_wst])

                def em_ab(e, k=k):
                    return e.activation(w2s[:, D * k:D * (k + 1)], wstage[:, 0:D],
                                        AF.Abs)
                comp("scalar", em_ab, reads=[B_wst], writes=[B_w2s])
                for nslc in range(2):
                    def em_mm(e, nslc=nslc, k=k):
                        return e.matmul(abanks[nslc], ones_col[:, :],
                                        w2s[:, D * k + 512 * nslc:
                                            D * k + 512 * (nslc + 1)],
                                        start=(k == 0), stop=(k == 31))
                    comp("tensor", em_mm, reads=[B_w2s, B_consts, B_abanks[nslc]],
                         writes=[B_abanks[nslc]])

                def em_sg_(e, k=k):
                    return e.sign(w2s[:, D * k:D * (k + 1)], wstage[:, 0:D])
                comp("scalar", em_sg_, reads=[B_wst, B_w2s], writes=[B_w2s])
            for nslc in range(2):
                def em_ae(e, nslc=nslc):
                    return e.mul(arow_st[:, :], abanks[nslc], 1.0 / FH)
                comp("scalar", em_ae, reads=[B_abanks[nslc]], writes=[B_arow])
                drain("scalar", arow_st[0:1, 508:512], B_arow)
                S.dma("sync",
                      bass.AP(alpha_dram.tensor, 512 * nslc, [[1, 1], [1, 512]]),
                      arow_st[:, :], reads=[B_arow], writes=[B_alpha_d])
            a_src = bass.AP(alpha_dram.tensor, 0, [[1, 128], [128, 8]])
            S.dma("sync", a2c[:, :], a_src, reads=[B_alpha_d], writes=[B_a2c],
                  allow_slow_non_contiguous=True)

            nhs_f = NHS if HS_LIMIT is None else HS_LIMIT
            for hi in range(nhs_f):
                t0 = HS * hi
                par = hi % 2
                xnf, bxnf = xn_f[par], B_xn_f[par]
                src = bass.AP(xn_dram.tensor, 128 + t0,
                              [[8 * XN_STR, 128], [XN_STR, 8], [1, HS]])
                S.dma("sync", xnf[:, :].rearrange("p (k t) -> p k t", k=8), src,
                      reads=[B_xn_dram], writes=[bxnf])
                srcs = bass.AP(s_dram.tensor, t0, [[8 * SEG, 128], [SEG, 8], [1, HS]])
                S.dma("sync", s_f[:, :].rearrange("p (m t) -> p m t", m=8), srcs,
                      reads=[B_s_dram], writes=[B_sf])

                for m in range(32):
                    pp = ps_proj if m % 2 == 0 else ps_proj2
                    bpp = B_proj[m % 2]
                    for k in range(8):
                        def em_mm(e, m=m, k=k, pp=pp, xnf=xnf):
                            return e.matmul(
                                pp[:, 0:HS],
                                w1s[:, FH * k + 128 * m:FH * k + 128 * m + 128],
                                xnf[:, HS * k:HS * (k + 1)],
                                start=(k == 0), stop=(k == 7))
                        comp("tensor", em_mm, reads=[B_w1s, bxnf, bpp], writes=[bpp])

                    def em_ev(e, m=m, pp=pp):
                        return e.activation(h1g[:, HS * m:HS * (m + 1)], pp[:, 0:HS],
                                            AF.Gelu, bias=b1c[:, m:m + 1],
                                            scale=a1c[:, m:m + 1])
                    comp("scalar", em_ev, reads=[bpp, B_bc, B_a1c], writes=[B_h1g])

                for m in range(8):
                    pp = ps_proj if m % 2 == 0 else ps_proj2
                    bpp = B_proj[m % 2]
                    for k in range(32):
                        def em_mm(e, m=m, k=k, pp=pp):
                            return e.matmul(
                                pp[:, 0:HS],
                                w2s[:, D * k + 128 * m:D * k + 128 * m + 128],
                                h1g[:, HS * k:HS * (k + 1)],
                                start=(k == 0), stop=(k == 31))
                        comp("tensor", em_mm, reads=[B_w2s, B_h1g, bpp], writes=[bpp])

                    def em_ev(e, m=m, pp=pp):
                        return e.activation(g_bf[:, HS * m:HS * (m + 1)], pp[:, 0:HS],
                                            AF.Silu, bias=b2c[:, m:m + 1],
                                            scale=a2c[:, m:m + 1])
                    comp("scalar", em_ev, reads=[bpp, B_bc, B_a2c], writes=[B_g])

                def em_sg2(e):
                    return e.tensor_mul(sg[:, :], g_bf[:, :], s_f[:, :])
                comp("vector", em_sg2, reads=[B_g, B_sf], writes=[B_sg])

                for c in range(2):
                    xf, bxf = x_f[c], B_x_f[c]
                    S.dma("sync", xf[:, :],
                          x_d[128 + t0 + 128 * c:128 + t0 + 128 * c + 128, :],
                          writes=[bxf])
                    for g2 in range(2):
                        for kk in range(4):
                            m = 4 * g2 + kk

                            def em_tp(e, m=m, kk=kk, c=c):
                                return e.transpose(
                                    ps_t[:, 128 * kk:128 * kk + 128],
                                    sg[:, HS * m + 128 * c:HS * m + 128 * c + 128],
                                    id_b[:, :])
                            comp("tensor", em_tp, reads=[B_sg, B_consts], writes=[B_t])

                        def em_o(e, g2=g2, xf=xf):
                            return e.tensor_add(o_f[:, 512 * g2:512 * (g2 + 1)],
                                                xf[:, 512 * g2:512 * (g2 + 1)],
                                                ps_t[:, :])
                        comp("vector", em_o, reads=[B_t, bxf], writes=[B_of])
                    drain("vector", o_f[:, D - 4:D], B_of)
                    S.dma("sync", out_d[t0 + 128 * c:t0 + 128 * c + 128, :], o_f[:, :],
                          reads=[B_of], writes=[B_out])

            S.barrier(allbufs)

        S.finalize()
        blk = ent(nc.Block())
        for ename, meth in (("tensor", blk.tensor), ("vector", blk.vector),
                            ("scalar", blk.scalar), ("sync", blk.sync),
                            ("gpsimd", blk.gpsimd)):
            def body(eng, ename=ename):
                S.stream(ename, eng, sems)
            meth(body)

    return nc


def _host_inputs(inputs):
    f32 = np.float32
    x = np.asarray(inputs["L0_attn_x"], f32)
    B = x.shape[0]
    shared = {}
    shared["w1t"] = np.ascontiguousarray(inputs["L0_ffn_w1"].T, f32)
    shared["w2t"] = np.ascontiguousarray(inputs["L0_ffn_w2"].T, f32)
    shared["b1c"] = np.ascontiguousarray(np.asarray(inputs["L0_ffn_b1"], f32)
                                         .reshape(32, 128).T)
    shared["b2c"] = np.ascontiguousarray(np.asarray(inputs["L0_ffn_b2"], f32)
                                         .reshape(8, 128).T)
    shared["boc"] = np.ascontiguousarray(np.asarray(inputs["L0_attn_bo"], f32)
                                         .reshape(8, 128).T)
    shared["ident"] = np.eye(128, dtype=f32)
    si = np.arange(128)[:, None]
    ti = np.arange(128)[None, :]
    shared["MB1"] = np.where(si <= ti, 0.0, NEG).astype(f32)
    shared["MB2"] = np.where(si >= ti, 0.0, NEG).astype(f32)
    shared["fconsts"] = np.ascontiguousarray(
        np.broadcast_to(np.asarray(FCONSTS, f32)[None, :], (128, len(FCONSTS))))
    Wo = np.asarray(inputs["L0_attn_Wo"], f32)

    def dirset(pref):
        d = {}
        d["Win"] = np.ascontiguousarray(np.asarray(inputs[f"L0_attn_{pref}Win"], f32))
        d["Wout"] = np.ascontiguousarray(np.asarray(inputs[f"L0_attn_{pref}Wout"], f32))
        cw = np.asarray(inputs[f"L0_attn_{pref}convw"], f32)[:, 0, :]
        d["wxs"] = np.ascontiguousarray(
            cw[:, :2048].reshape(4, 16, 128).transpose(2, 1, 0).reshape(128, 64))
        d["wBC"] = np.ascontiguousarray(
            np.concatenate([cw[:, 2048:2112].T, cw[:, 2112:2176].T], 1))
        cb = np.asarray(inputs[f"L0_attn_{pref}convb"], f32)
        d["bxs"] = np.ascontiguousarray(cb[:2048].reshape(16, 128).T)
        d["bBC"] = np.ascontiguousarray(np.stack([cb[2048:2112], cb[2112:2176]], 1))
        d["dtb"] = np.asarray(inputs[f"L0_attn_{pref}dtb"], f32).reshape(32, 1).copy()
        d["Alog"] = np.asarray(inputs[f"L0_attn_{pref}Alog"], f32).reshape(32, 1).copy()
        Dv = np.asarray(inputs[f"L0_attn_{pref}D"], f32)
        d16 = np.empty((128, 16), f32)
        for par in range(2):
            for j in range(16):
                d16[64 * par:64 * par + 64, j] = Dv[2 * j + par]
        d["D16"] = d16
        nw = np.asarray(inputs[f"L0_attn_{pref}normw"], f32)
        d["normw"] = np.ascontiguousarray(nw.reshape(16, 128).T)
        return d

    fset, bset = dirset("f"), dirset("b")
    Wo_f = np.ascontiguousarray(Wo[:D])
    Wo_b = np.ascontiguousarray(Wo[D:])
    zeros128 = np.zeros((128, D), f32)

    in_maps = []
    for c in range(8):
        b, half = c // 2, c % 2
        m = dict(shared)
        if half == 0:
            xp = np.concatenate([zeros128, x[b][:SEG + 128]], 0)
            p1, p2 = fset, bset
            m["Wo1"], m["Wo2"] = Wo_f, Wo_b
        else:
            xr = x[b][::-1]
            xp = np.concatenate([zeros128, xr[:SEG + 128]], 0)
            p1, p2 = bset, fset
            m["Wo1"], m["Wo2"] = Wo_b, Wo_f
        m["x"] = np.ascontiguousarray(xp)
        for key, val in p1.items():
            m["p1" + key] = val
        for key, val in p2.items():
            m["p2" + key] = val
        in_maps.append(m)
    return in_maps, B


def kernel(**inputs):
    in_maps, B = _host_inputs(inputs)
    nc = build_program()
    res = run_bass_kernel_spmd(nc, in_maps, core_ids=list(range(8)))
    out = np.empty((B, L, D), np.float32)
    for c in range(8):
        b, half = c // 2, c % 2
        o = np.asarray(res.results[c]["out"], np.float32)
        if half == 0:
            out[b, :SEG] = o
        else:
            out[b, SEG:] = o[::-1]
    return out


# version 05 (final)
# speedup vs baseline: 1.0893x; 1.0893x over previous
import contextlib
import numpy as np
import ml_dtypes
import concourse.bass as bass
import concourse.mybir as mybir
from concourse.bass_utils import run_bass_kernel_spmd

F32 = mybir.dt.float32
BF16 = mybir.dt.bfloat16
AF = mybir.ActivationFunctionType
ALU = mybir.AluOpType

D = 1024
DP = 4256
DI = 2048
L, SEG = 8192, 4096
CH = 128
HS = 256
NHS = SEG // HS
NCHUNK = SEG // CH
FH = 4096
XN_STR = SEG + 256
NEG = -1e30

ENGS = ("tensor", "vector", "scalar", "sync", "gpsimd")

DEBUG_TAPS = {}
HALO_ZERO_HACK = False
TAP_CHUNK = 0
TAP_Q = 0
TAP_PASS = 1
TAP_HS = 0
HS_LIMIT = None

FCONSTS = [-1.0, 1.0, 0.0, 1e-5, 1.0 / D, -1.0 / D, 1.0 / DI, 1.0 / FH]


class Buf:
    __slots__ = ("name", "last_write", "readers")

    def __init__(self, name):
        self.name = name
        self.last_write = None
        self.readers = []


class Op:
    __slots__ = ("eng", "emit", "deps", "kind", "queue", "consumed", "sem", "val",
                 "fence", "eidx", "waits")

    def __init__(self, eng, emit, kind, queue):
        self.eng, self.emit, self.kind, self.queue = eng, emit, kind, queue
        self.deps = []
        self.consumed = False
        self.sem = None
        self.val = None
        self.fence = False


class Sched:
    def __init__(self):
        self.eng_ops = {e: [] for e in ENGS}
        self.last_dma = {}

    def add(self, eng, emit, reads=(), writes=(), kind="comp", fence=False):
        queue = eng if kind == "dma" else None
        op = Op(eng, emit, kind, queue)
        op.fence = fence
        deps, seen = [], set()
        for b in reads:
            if b.last_write is not None and id(b.last_write) not in seen:
                seen.add(id(b.last_write))
                deps.append(b.last_write)
        for b in writes:
            if b.last_write is not None and id(b.last_write) not in seen:
                seen.add(id(b.last_write))
                deps.append(b.last_write)
            for r in b.readers:
                if id(r) not in seen and r is not op:
                    seen.add(id(r))
                    deps.append(r)
        op.deps = deps
        self.eng_ops[eng].append(op)
        for b in reads:
            b.readers.append(op)
        for b in writes:
            b.last_write = op
            b.readers = []
        if kind == "dma":
            self.last_dma.setdefault(eng, []).append(op)
            if len(self.last_dma[eng]) > 8:
                self.last_dma[eng].pop(0)
        return op

    def dma(self, queue, out, in_, reads=(), writes=(), **kw):
        def emit(e, out=out, in_=in_, kw=kw):
            return e.dma_start(out=out, in_=in_, **kw)
        return self.add(queue, emit, reads=reads, writes=writes, kind="dma")

    def barrier(self, bufs):
        lasts = []
        for e in ENGS:
            for o in reversed(self.eng_ops[e]):
                if o.kind == "comp":
                    lasts.append(o)
                    break
        for ops_ in self.last_dma.values():
            lasts.extend(ops_)
        for e in ENGS:
            if self.eng_ops[e] and self.eng_ops[e][-1].kind == "coll":
                lasts.append(self.eng_ops[e][-1])
        for e in ENGS:
            op = Op(e, lambda eng: eng.nop(), "comp", None)
            op.deps = [d for d in lasts if not (d.eng == e and d.kind == "comp")]
            self.eng_ops[e].append(op)
        for b in bufs:
            b.last_write = None
            b.readers = []

    def finalize(self):
        for e in ENGS:
            for i, op in enumerate(self.eng_ops[e]):
                op.eidx = i
        for e in ENGS:
            for op in self.eng_ops[e]:
                if op.kind != "comp" or op.fence:
                    continue
                for d in op.deps:
                    if (d.kind == "comp" and d.eng == op.eng
                            and op.eidx - d.eidx <= 2):
                        op.fence = True
                        break
        for e in ENGS:
            for op in self.eng_ops[e]:
                for d in op.deps:
                    if d.kind == "comp" and d.eng == op.eng and not op.fence:
                        continue
                    d.consumed = True
        counters = {}
        qidx = {}
        for e in ENGS:
            for op in self.eng_ops[e]:
                if op.kind == "dma":
                    i = qidx.get(op.queue, 0)
                    qidx[op.queue] = i + 1
                    op.sem = f"dma_{op.queue}_{i % 8}"
                    counters[op.sem] = counters.get(op.sem, 0) + 16
                    op.val = counters[op.sem]
                elif op.kind == "coll":
                    op.sem = "coll"
                    counters[op.sem] = counters.get(op.sem, 0) + 1
                    op.val = counters[op.sem]
                elif op.consumed:
                    op.sem = e
                    counters[op.sem] = counters.get(op.sem, 0) + 1
                    op.val = counters[op.sem]

    def stream(self, e, eng, sems):
        watermark = {}
        for op in self.eng_ops[e]:
            for d in op.deps:
                if d.kind == "comp" and d.eng == e and not op.fence:
                    continue
                if watermark.get(d.sem, 0) >= d.val:
                    continue
                eng.wait_ge(sems[d.sem], d.val)
                watermark[d.sem] = d.val
            inst = op.emit(eng)
            if op.sem is not None:
                if op.kind == "coll":
                    inst.then_inc(sems[op.sem])
                elif op.kind == "dma":
                    inst.then_inc(sems[op.sem], 16)
                else:
                    inst.then_inc(sems[op.sem], 1)


def build_program():
    nc = bass.Bass("TRN2", target_bir_lowering=False)
    S = Sched()
    allbufs = []

    def mkbuf(name):
        b = Buf(name)
        allbufs.append(b)
        return b

    def din(name, shape, dtype=F32):
        return nc.dram_tensor(name, list(shape), dtype, kind="ExternalInput").ap()

    x_d = din("x", [SEG + 256, D])
    Win_d = {1: din("p1Win", [D, DP]), 2: din("p2Win", [D, DP])}
    Wout_d = {1: din("p1Wout", [DI, D]), 2: din("p2Wout", [DI, D])}
    Wo_d = {1: din("Wo1", [D, D]), 2: din("Wo2", [D, D])}
    wxs_d = {1: din("p1wxs", [128, 64]), 2: din("p2wxs", [128, 64])}
    wBC_d = {1: din("p1wBC", [64, 8]), 2: din("p2wBC", [64, 8])}
    bxs_d = {1: din("p1bxs", [128, 16]), 2: din("p2bxs", [128, 16])}
    bBC_d = {1: din("p1bBC", [64, 2]), 2: din("p2bBC", [64, 2])}
    dtb_d = {1: din("p1dtb", [32, 1]), 2: din("p2dtb", [32, 1])}
    Alog_d = {1: din("p1Alog", [32, 1]), 2: din("p2Alog", [32, 1])}
    D16_d = {1: din("p1D16", [128, 16]), 2: din("p2D16", [128, 16])}
    normw_d = {1: din("p1normw", [128, 16]), 2: din("p2normw", [128, 16])}
    w1t_d = din("w1t", [D, FH])
    w2t_d = din("w2t", [FH, D])
    b1c_d = din("b1c", [128, 32])
    b2c_d = din("b2c", [128, 8])
    boc_d = din("boc", [128, 8])
    ident_d = din("ident", [128, 128])
    MB_d = {1: din("MB1", [128, 128]), 2: din("MB2", [128, 128])}
    fconst_d = din("fconsts", [128, len(FCONSTS)])

    out_d = nc.dram_tensor("out", [SEG, D], F32, kind="ExternalOutput").ap()
    taps = {}
    for tname, (tshape, tdt) in DEBUG_TAPS.items():
        taps[tname] = nc.dram_tensor("tap_" + tname, list(tshape), tdt,
                                     kind="ExternalOutput").ap()

    xn_dram = nc.dram_tensor("xn_dram", [128, 8 * XN_STR], BF16, kind="Internal").ap()
    scd_dram = {p: nc.dram_tensor(f"scd{p}", [96, SEG], F32, kind="Internal").ap()
                for p in (1, 2)}
    u_dram = {p: nc.dram_tensor(f"ud{p}", [32, SEG], F32, kind="Internal").ap()
              for p in (1, 2)}
    dec_dram = {p: nc.dram_tensor(f"dd{p}", [NCHUNK, 32], F32, kind="Internal").ap()
                for p in (1, 2)}
    P_dram = nc.dram_tensor("P_dram", [128, 8 * SEG], BF16, kind="Internal").ap()
    s_dram = nc.dram_tensor("s_dram", [128, 8 * SEG], BF16, kind="Internal").ap()
    alpha_dram = nc.dram_tensor("alpha_dram", [1, FH], F32, kind="Internal").ap()
    coll_in = nc.dram_tensor("coll_in", [64, 2048], F32, kind="Internal").ap()
    coll_out = nc.dram_tensor("coll_out", [64, 2048], F32, kind="Internal").ap()

    with contextlib.ExitStack() as st:
        ent = st.enter_context

        sems = {}
        for e in ENGS:
            sems[e] = ent(nc.semaphore("sem_" + e))
        for q in ("sync", "gpsimd"):
            for ln in range(8):
                sems[f"dma_{q}_{ln}"] = ent(nc.semaphore(f"dsem_{q}{ln}"))
        sems["coll"] = ent(nc.semaphore("csem"))

        id_f = ent(nc.sbuf_tensor("id_f", [128, 128], F32))
        id_b = ent(nc.sbuf_tensor("id_b", [128, 128], BF16))
        ones_col = ent(nc.sbuf_tensor("ones_col", [128, 1], BF16))
        ones_row = ent(nc.sbuf_tensor("ones_row", [1, 128], BF16))
        wxs = {p: ent(nc.sbuf_tensor(f"wxs{p}", [128, 64], F32)) for p in (1, 2)}
        wBC = {p: ent(nc.sbuf_tensor(f"wBC{p}", [64, 8], F32)) for p in (1, 2)}
        bxs = {p: ent(nc.sbuf_tensor(f"bxs{p}", [128, 16], F32)) for p in (1, 2)}
        bBC = {p: ent(nc.sbuf_tensor(f"bBC{p}", [64, 2], F32)) for p in (1, 2)}
        dtbc = {p: ent(nc.sbuf_tensor(f"dtbc{p}", [32, 1], F32)) for p in (1, 2)}
        Acol = {p: ent(nc.sbuf_tensor(f"Acol{p}", [32, 1], F32)) for p in (1, 2)}
        D16 = {p: ent(nc.sbuf_tensor(f"D16_{p}", [128, 16], BF16)) for p in (1, 2)}
        normw = {p: ent(nc.sbuf_tensor(f"normw{p}", [128, 16], F32)) for p in (1, 2)}
        boc_sb = ent(nc.sbuf_tensor("boc_sb", [128, 8], F32))
        states_f = ent(nc.sbuf_tensor("states_f", [64, 2048], F32))
        states_b = ent(nc.sbuf_tensor("states_b", [64, 2048], BF16))

        ps_main = ent(nc.psum_tensor("ps_main", [128, 2048], F32))
        ps_proj = ent(nc.psum_tensor("ps_proj", [128, 512], F32))
        ps_proj2 = ent(nc.psum_tensor("ps_proj2", [128, 512], F32))
        ps_aux = ent(nc.psum_tensor("ps_aux", [128, 512], F32))
        ps_tg = ent(nc.psum_tensor("ps_tg", [128, 512], F32))
        ps_t = ps_tg[:, 0:256].bitcast(BF16)
        ps_gs = ps_tg[:, 256:384]

        B_main = [mkbuf(f"ps_main{i}") for i in range(4)]
        B_proj = [mkbuf("ps_proj"), mkbuf("ps_proj2")]
        B_aux = mkbuf("ps_aux")
        B_t = mkbuf("ps_tg")
        B_gs = B_t
        B_stf = mkbuf("states_f")
        B_stb = mkbuf("states_b")
        B_consts = mkbuf("consts")
        B_xn_dram = mkbuf("xn_dram")
        B_scd = {p: mkbuf(f"scd{p}") for p in (1, 2)}
        B_u = {p: mkbuf(f"u{p}") for p in (1, 2)}
        B_dec = {p: mkbuf(f"dec{p}") for p in (1, 2)}
        B_P_dram = mkbuf("P_dram")
        B_s_dram = mkbuf("s_dram")
        B_coll = mkbuf("coll")
        B_out = mkbuf("out")
        B_alpha_d = mkbuf("alpha_d")

        def comp(eng, emit, reads=(), writes=(), fence=False):
            return S.add(eng, emit, reads=reads, writes=writes, kind="comp",
                         fence=fence)

        def tap(name, ap, rbuf):
            if name in taps:
                S.dma("sync", taps[name], ap, reads=[rbuf])

        drain_scr = ent(nc.sbuf_tensor("drain_scr", [128, 16], F32))

        def drain(eng, tail_ap, rbuf):
            n = tail_ap.shape[0]

            def em(e, tail_ap=tail_ap, n=n):
                if eng == "scalar":
                    return e.copy(drain_scr[0:n, 0:tail_ap.shape[1]], tail_ap)
                return e.tensor_copy(drain_scr[0:n, 0:tail_ap.shape[1]], tail_ap)
            comp(eng, em, reads=[rbuf], writes=[rbuf])

        fconst_sb = ent(nc.sbuf_tensor("fconst_sb", [128, len(FCONSTS)], F32))
        S.dma("sync", fconst_sb[:, :], fconst_d, writes=[B_consts])
        for i, v in enumerate(FCONSTS):
            nc.const_aps.register(F32, v, fconst_sb[:, i:i + 1])
        S.dma("sync", id_f[:, :], ident_d, writes=[B_consts])
        S.dma("sync", boc_sb[:, :], boc_d, writes=[B_consts])
        for p in (1, 2):
            S.dma("sync", wxs[p][:, :], wxs_d[p], writes=[B_consts])
            S.dma("sync", wBC[p][:, :], wBC_d[p], writes=[B_consts])
            S.dma("sync", bxs[p][:, :], bxs_d[p], writes=[B_consts])
            S.dma("sync", bBC[p][:, :], bBC_d[p], writes=[B_consts])
            S.dma("sync", dtbc[p][:, :], dtb_d[p], writes=[B_consts])
            S.dma("sync", normw[p][:, :], normw_d[p], writes=[B_consts])
            S.dma("sync", Acol[p][:, :], Alog_d[p], writes=[B_consts])
            S.dma("gpsimd", D16[p][:, :], D16_d[p], writes=[B_consts])
        comp("vector", lambda e: e.tensor_copy(id_b[:, :], id_f[:, :]),
             reads=[B_consts], writes=[B_consts])
        comp("vector", lambda e: e.memset(ones_col[:, :], 1.0), writes=[B_consts])
        comp("vector", lambda e: e.memset(ones_row[:, :], 1.0), writes=[B_consts])
        for p in (1, 2):
            def em_exp(e, p=p):
                return e.activation(Acol[p][:, :], Acol[p][:, :], AF.Exp)
            comp("scalar", em_exp, reads=[B_consts], writes=[B_consts])

            def em_neg(e, p=p):
                return e.tensor_scalar_mul(Acol[p][:, :], Acol[p][:, :], -1.0)
            comp("vector", em_neg, reads=[B_consts], writes=[B_consts])

        S.barrier(allbufs)

        with contextlib.ExitStack() as pst:
            pent = pst.enter_context
            x_tm = [pent(nc.sbuf_tensor(f"x_tm{i}", [128, D], F32)) for i in range(2)]
            B_x_tm = [mkbuf(f"x_tm{i}") for i in range(2)]
            lnsc = pent(nc.sbuf_tensor("lnsc", [128, D], BF16))
            B_lnsc = mkbuf("lnsc")
            lncol = pent(nc.sbuf_tensor("lncol", [128, 4], F32))
            B_lncol = mkbuf("lncol")
            xn_tm = pent(nc.sbuf_tensor("xn_tm", [128, D], BF16))
            B_xn_tm = mkbuf("xn_tm")
            xn_ch = pent(nc.sbuf_tensor("xn_ch", [128, 8 * 128], BF16))
            B_xn_ch = mkbuf("xn_ch")
            Windt = {p: pent(nc.sbuf_tensor(f"Windt{p}", [128, 8 * 32], BF16))
                     for p in (1, 2)}
            B_windt = mkbuf("windt")
            dtraw = {p: pent(nc.sbuf_tensor(f"dtraw{p}", [32, SEG], F32)) for p in (1, 2)}
            B_dtraw = {p: mkbuf(f"dtraw{p}") for p in (1, 2)}

            for p in (1, 2):
                for k in range(8):
                    src = bass.AP(Win_d[p].tensor, (128 * k) * DP + 4224,
                                  [[DP, 128], [1, 32]])
                    S.dma("gpsimd", Windt[p][:, 32 * k:32 * k + 32], src,
                          writes=[B_windt])

            for ci in range(34):
                par = ci % 2
                xt = x_tm[par]
                bxt = B_x_tm[par]
                S.dma("sync", xt[:, :], x_d[128 * ci:128 * ci + 128, :], writes=[bxt])

                def em_sum(e, xt=xt):
                    return e.activation(lnsc[:, :], xt[:, :], AF.Copy,
                                        accum_out=lncol[:, 0:1])
                comp("scalar", em_sum, reads=[bxt], writes=[B_lnsc, B_lncol])

                def em_nm(e):
                    return e.mul(lncol[:, 1:2], lncol[:, 0:1], -1.0 / D)
                comp("scalar", em_nm, reads=[B_lncol], writes=[B_lncol], fence=True)

                def em_sq(e, xt=xt):
                    return e.activation(lnsc[:, :], xt[:, :], AF.Square,
                                        bias=lncol[:, 1:2], accum_out=lncol[:, 2:3])
                comp("scalar", em_sq, reads=[bxt, B_lncol], writes=[B_lnsc, B_lncol], fence=True)

                def em_sd(e):
                    return e.activation(lncol[:, 3:4], lncol[:, 2:3], AF.Sqrt,
                                        scale=1.0 / D, bias=1e-5)
                comp("scalar", em_sd, reads=[B_lncol], writes=[B_lncol], fence=True)

                def em_rc(e):
                    return e.reciprocal(lncol[:, 3:4], lncol[:, 3:4])
                comp("vector", em_rc, reads=[B_lncol], writes=[B_lncol])

                def em_xn(e, xt=xt):
                    return e.tensor_scalar(xn_tm[:, :], xt[:, :], lncol[:, 1:2],
                                           lncol[:, 3:4], ALU.add, ALU.mult)
                comp("vector", em_xn, reads=[bxt, B_lncol], writes=[B_xn_tm], fence=True)

                for g in range(2):
                    for kk in range(4):
                        k = 4 * g + kk

                        def em_tp(e, k=k, kk=kk):
                            return e.transpose(ps_t[:, 128 * kk:128 * kk + 128],
                                               xn_tm[:, 128 * k:128 * k + 128],
                                               id_b[:, :])
                        comp("tensor", em_tp, reads=[B_xn_tm, B_consts], writes=[B_t])

                    def em_ev(e, g=g):
                        return e.tensor_copy(xn_ch[:, 512 * g:512 * g + 512], ps_t[:, :])
                    comp("vector", em_ev, reads=[B_t], writes=[B_xn_ch])
                drain("vector", xn_ch[:, 1020:1024], B_xn_ch)
                dst = bass.AP(xn_dram.tensor, 128 * ci,
                              [[8 * XN_STR, 128], [XN_STR, 8], [1, 128]])
                S.dma("sync", dst, xn_ch[:, :].rearrange("p (k t) -> p k t", k=8),
                      reads=[B_xn_ch], writes=[B_xn_dram])

                if 1 <= ci <= 32:
                    tloc = 128 * (ci - 1)
                    for p in (1, 2):
                        for k in range(8):
                            def em_mm(e, p=p, k=k):
                                return e.matmul(ps_gs[0:32, :],
                                                Windt[p][:, 32 * k:32 * k + 32],
                                                xn_ch[:, 128 * k:128 * k + 128],
                                                start=(k == 0), stop=(k == 7))
                            comp("tensor", em_mm, reads=[B_windt, B_xn_ch],
                                 writes=[B_gs])

                        def em_dt(e, p=p, tloc=tloc):
                            return e.tensor_copy(dtraw[p][:, tloc:tloc + 128],
                                                 ps_gs[0:32, :])
                        comp("vector", em_dt, reads=[B_gs], writes=[B_dtraw[p]])

            sp_a = pent(nc.sbuf_tensor("sp_a", [32, SEG], F32))
            B_spa = mkbuf("sp_a")
            sp_m = pent(nc.sbuf_tensor("sp_m", [32, SEG], F32))
            B_spm = mkbuf("sp_m")
            csum = pent(nc.sbuf_tensor("csum", [32, SEG + 1], F32))
            B_csum = mkbuf("csum")
            csb = pent(nc.sbuf_tensor("csb", [32, SEG + 1], F32))
            B_csb = mkbuf("csb")
            ncsum = pent(nc.sbuf_tensor("ncsum", [32, SEG + 1], F32))
            B_ncsum = mkbuf("ncsum")
            sc32 = pent(nc.sbuf_tensor("sc32", [32, 128], F32))
            B_sc32 = mkbuf("sc32")
            dec32 = pent(nc.sbuf_tensor("dec32", [32, 32], F32))
            B_dec32 = mkbuf("dec32")

            for p in (1, 2):
                rw, brw = dtraw[p], B_dtraw[p]

                def em1(e, rw=rw, p=p):
                    return e.activation(sp_a[:, :], rw[:, :], AF.Abs, bias=dtbc[p][:, :])
                comp("scalar", em1, reads=[brw, B_consts], writes=[B_spa])

                def em2(e):
                    return e.activation(sp_a[:, :], sp_a[:, :], AF.Exp, scale=-1.0)
                comp("scalar", em2, reads=[B_spa], writes=[B_spa])

                def em3(e):
                    return e.activation(sp_a[:, :], sp_a[:, :], AF.Ln, bias=1.0)
                comp("scalar", em3, reads=[B_spa], writes=[B_spa])

                def em4(e, rw=rw, p=p):
                    return e.tensor_scalar(sp_m[:, :], rw[:, :], dtbc[p][:, :], 0.0,
                                           ALU.add, ALU.max)
                comp("vector", em4, reads=[brw, B_consts], writes=[B_spm])

                def em5(e):
                    return e.tensor_add(sp_m[:, :], sp_m[:, :], sp_a[:, :])
                comp("vector", em5, reads=[B_spa, B_spm], writes=[B_spm])
                drain("vector", sp_m[0:32, SEG - 4:SEG], B_spm)
                S.dma("sync", bass.AP(scd_dram[p].tensor, 32 * SEG, [[SEG, 32], [1, SEG]]),
                      sp_m[:, :], reads=[B_spm], writes=[B_scd[p]])

                def em6(e, p=p):
                    return e.tensor_scalar_mul(sp_a[:, :], sp_m[:, :], Acol[p][:, :])
                comp("vector", em6, reads=[B_spm, B_consts], writes=[B_spa])

                def em7(e):
                    return e.memset(sp_m[:, :], 0.0)
                comp("vector", em7, writes=[B_spm])

                def em8(e):
                    return e.tensor_tensor_scan(csum[:, 1:SEG + 1], sp_a[:, :],
                                                sp_m[:, :], 0.0, ALU.add, ALU.add)
                comp("vector", em8, reads=[B_spa, B_spm], writes=[B_csum])

                def em9(e):
                    return e.memset(csum[:, 0:1], 0.0)
                comp("vector", em9, writes=[B_csum])

                if p == 2:
                    def emb1(e):
                        return e.tensor_scalar(csb[:, 0:SEG], csum[:, 1:SEG + 1],
                                               csum[:, SEG:SEG + 1], -1.0,
                                               ALU.subtract, ALU.mult)
                    comp("vector", emb1, reads=[B_csum], writes=[B_csb], fence=True)

                    def emb2(e):
                        return e.tensor_add(csb[:, 0:SEG], csb[:, 0:SEG], sp_a[:, :])
                    comp("vector", emb2, reads=[B_spa, B_csb], writes=[B_csb])

                    def emb3(e):
                        return e.memset(csb[:, SEG:SEG + 1], 0.0)
                    comp("vector", emb3, writes=[B_csb])

                CS = csum if p == 1 else csb
                B_CS = B_csum if p == 1 else B_csb
                voff = 1 if p == 1 else 0
                drain("vector", CS[0:32, SEG - 3:SEG + 1], B_CS)
                S.dma("sync", bass.AP(scd_dram[p].tensor, 0, [[SEG, 32], [1, SEG]]),
                      CS[:, voff:voff + SEG], reads=[B_CS], writes=[B_scd[p]])

                def em10(e, CS=None):
                    return e.tensor_scalar_mul(ncsum[:, :], CS[:, :], -1.0)
                def em10b(e, CS=CS):
                    return e.tensor_scalar_mul(ncsum[:, :], CS[:, :], -1.0)
                comp("vector", em10b, reads=[B_CS], writes=[B_ncsum])
                if f"ncsum{p}" in taps:
                    S.dma("sync", taps[f"ncsum{p}"], ncsum[:, :], reads=[B_ncsum])

                for k in range(NCHUNK):
                    if p == 1:
                        vc, entry, exit_ = 1 + 128 * k, 128 * k, 128 * (k + 1)
                    else:
                        vc, entry, exit_ = 128 * k, 128 * (k + 1), 128 * k

                    def em_w(e, vc=vc, exit_=exit_, CS=CS):
                        return e.activation(sc32[:, :], CS[:, vc:vc + 128], AF.Exp,
                                            scale=-1.0, bias=CS[:, exit_:exit_ + 1])
                    comp("scalar", em_w, reads=[B_CS], writes=[B_sc32])
                    drain("scalar", sc32[0:32, 124:128], B_sc32)
                    S.dma("sync",
                          bass.AP(scd_dram[p].tensor, 64 * SEG + 128 * k,
                                  [[SEG, 32], [1, 128]]),
                          sc32[:, :], reads=[B_sc32], writes=[B_scd[p]])

                    def em_u(e, vc=vc, entry=entry, CS=CS):
                        return e.activation(sc32[:, :], CS[:, vc:vc + 128], AF.Exp,
                                            bias=ncsum[:, entry:entry + 1])
                    comp("scalar", em_u, reads=[B_CS, B_ncsum], writes=[B_sc32])
                    drain("scalar", sc32[0:32, 124:128], B_sc32)
                    S.dma("sync",
                          bass.AP(u_dram[p].tensor, 128 * k, [[SEG, 32], [1, 128]]),
                          sc32[:, :], reads=[B_sc32], writes=[B_u[p]])

                ex0, en0 = (128, 0) if p == 1 else (0, 128)

                def em_d1(e, ex0=ex0, en0=en0, CS=CS):
                    ap_ex = CS[:, ex0::128][:, 0:32]
                    ap_en = CS[:, en0::128][:, 0:32]
                    return e.tensor_sub(dec32[:, :], ap_ex, ap_en)
                comp("vector", em_d1, reads=[B_CS], writes=[B_dec32], fence=True)

                def em_d2(e):
                    return e.activation(dec32[:, :], dec32[:, :], AF.Exp)
                comp("scalar", em_d2, reads=[B_dec32], writes=[B_dec32])
                drain("scalar", dec32[0:32, 28:32], B_dec32)
                S.dma("sync", bass.AP(dec_dram[p].tensor, 0, [[1, 32], [32, 32]]),
                      dec32[:, :], reads=[B_dec32], writes=[B_dec[p]],
                      allow_slow_non_contiguous=True)

                if f"csum{p}" in taps:
                    S.dma("sync", taps[f"csum{p}"], CS[:, :], reads=[B_CS])
                if f"udram{p}" in taps:
                    S.dma("sync", taps[f"udram{p}"], u_dram[p], reads=[B_u[p]])
                if f"scddram{p}" in taps:
                    S.dma("sync", taps[f"scddram{p}"], scd_dram[p], reads=[B_scd[p]])
            if "xn" in taps:
                S.dma("sync", taps["xn"], xn_dram, reads=[B_xn_dram])
            S.barrier(allbufs)

        with contextlib.ExitStack() as dst_:
            dent = dst_.enter_context
            DPL = 4224
            Win_bf = dent(nc.sbuf_tensor("Win_bf", [128, 8 * DPL], BF16))
            B_Win = mkbuf("Win_bf")
            Wout_bf = dent(nc.sbuf_tensor("Wout_bf", [128, 16 * D], BF16))
            B_Wout = mkbuf("Wout_bf")
            Wo_bf = dent(nc.sbuf_tensor("Wo_bf", [128, 16 * D], BF16))
            B_Wo = mkbuf("Wo_bf")
            xn_hs = [dent(nc.sbuf_tensor(f"xn_hs{i}", [128, 8 * HS], BF16))
                     for i in range(2)]
            B_xn_hs = [mkbuf(f"xn_hs{i}") for i in range(2)]
            zs_hs = dent(nc.sbuf_tensor("zs_hs", [128, 16 * HS], BF16))
            B_zs = mkbuf("zs_hs")
            xbc_xs = dent(nc.sbuf_tensor("xbc_xs", [128, 16 * (HS + 3)], BF16))
            B_xbc = mkbuf("xbc_xs")
            xbc_B = dent(nc.sbuf_tensor("xbc_B", [64, HS + 3], BF16))
            B_xbcB = mkbuf("xbc_B")
            xbc_C = dent(nc.sbuf_tensor("xbc_C", [64, HS + 3], BF16))
            B_xbcC = mkbuf("xbc_C")
            cxs = dent(nc.sbuf_tensor("cxs", [128, 16 * HS], BF16))
            B_cxs = mkbuf("cxs")
            convacc = dent(nc.sbuf_tensor("convacc", [128, HS], BF16))
            B_cacc = mkbuf("convacc")
            cB = dent(nc.sbuf_tensor("cB", [64, HS], BF16))
            B_cB = mkbuf("cB")
            cC = dent(nc.sbuf_tensor("cC", [64, HS], BF16))
            B_cC = mkbuf("cC")
            small_srcs = [dent(nc.sbuf_tensor(f"small_src{i}", [128, 128], F32))
                          for i in range(2)]
            B_ssrcs = [mkbuf(f"small_src{i}") for i in range(2)]
            small_sbs = [dent(nc.sbuf_tensor(f"small_sb{i}", [128, 96], F32))
                         for i in range(2)]
            B_ssbs = [mkbuf(f"small_sb{i}") for i in range(2)]
            dtw_bfs = [dent(nc.sbuf_tensor(f"dtw_bf{i}", [128, 64], BF16))
                       for i in range(2)]
            B_dtws = [mkbuf(f"dtw_bf{i}") for i in range(2)]
            CTs = [dent(nc.sbuf_tensor(f"CT{i}", [128, 512], F32)) for i in range(2)]
            B_CTs = [mkbuf(f"CT{i}") for i in range(2)]
            E_sb = dent(nc.sbuf_tensor("E_sb", [128, 512], BF16))
            B_E = mkbuf("E_sb")
            S_sb = E_sb
            B_S = B_E

            G_sb = dent(nc.sbuf_tensor("G_sb", [128, 128], BF16))
            B_G = mkbuf("G_sb")
            Bstage = dent(nc.sbuf_tensor("Bstage", [128, 128], BF16))
            B_Bst = mkbuf("Bstage")
            B_tm = dent(nc.sbuf_tensor("B_tm", [128, 64], BF16))
            B_Btm = mkbuf("B_tm")
            xs_tm = dent(nc.sbuf_tensor("xs_tm", [128, 2048], BF16))
            B_xstm = mkbuf("xs_tm")
            Xt = xs_tm
            B_Xt = B_xstm
            Xh = dent(nc.sbuf_tensor("Xh", [128, 2048], BF16))
            B_Xh = mkbuf("Xh")
            u64 = dent(nc.sbuf_tensor("u64", [64, 512], BF16))
            B_u64 = mkbuf("u64")
            dec64_ = dent(nc.sbuf_tensor("dec64", [64, 32], F32))
            B_dec64_ = mkbuf("dec64")
            Chat = dent(nc.sbuf_tensor("Chat", [64, 512], BF16))
            B_Chat = mkbuf("Chat")
            dxs = dent(nc.sbuf_tensor("dxs", [128, 512], BF16))
            B_dxs = mkbuf("dxs")
            y1 = CTs[0]
            B_y1 = B_CTs[0]
            ySq = dxs
            B_ySq = B_dxs
            yg_hs = xbc_xs[:, 0:16 * HS]
            B_yg = B_xbc
            xtail = dent(nc.sbuf_tensor("xtail", [128, 48], BF16))
            B_xtail = mkbuf("xtail")
            btail = dent(nc.sbuf_tensor("btail", [64, 6], BF16))
            B_btail = mkbuf("btail")
            rstd_c = dent(nc.sbuf_tensor("rstd_c", [1, HS], F32))
            B_rstd = mkbuf("rstd_c")
            rstd_b = dent(nc.sbuf_tensor("rstd_b", [1, HS], BF16))
            B_rstdb = mkbuf("rstd_b")
            rstdx = dent(nc.sbuf_tensor("rstdx", [128, HS], BF16))
            B_rstdx = mkbuf("rstdx")
            P_sb = dent(nc.sbuf_tensor("P_sb", [128, 8 * HS], BF16))
            B_P = mkbuf("P_sb")
            P1_sb = dent(nc.sbuf_tensor("P1_sb", [128, 8 * HS], BF16))
            B_P1 = mkbuf("P1_sb")
            s_sb = P1_sb
            B_s = B_P1

            for dpass in (1, 2):
                rev = (dpass == 2)
                p = dpass
                for k in range(8):
                    S.dma("gpsimd", Win_bf[:, DPL * k:DPL * (k + 1)],
                          Win_d[p][128 * k:128 * (k + 1), 0:DPL], writes=[B_Win])
                for k in range(16):
                    S.dma("gpsimd", Wout_bf[:, D * k:D * (k + 1)],
                          Wout_d[p][128 * k:128 * (k + 1), :], writes=[B_Wout])
                for k in range(16):
                    def em_f(e, k=k, p=p):
                        return e.tensor_scalar_mul(Wout_bf[:, D * k:D * (k + 1)],
                                                   Wout_bf[:, D * k:D * (k + 1)],
                                                   normw[p][:, k:k + 1])
                    comp("vector", em_f, reads=[B_Wout, B_consts], writes=[B_Wout])
                if dpass == 2:
                    for k in range(8):
                        S.dma("gpsimd", Wo_bf[:, D * k:D * (k + 1)],
                              Wo_d[1][128 * k:128 * (k + 1), :], writes=[B_Wo])
                    for k in range(8):
                        S.dma("gpsimd", Wo_bf[:, D * (8 + k):D * (9 + k)],
                              Wo_d[2][128 * k:128 * (k + 1), :], writes=[B_Wo])

                if dpass == 1:
                    comp("vector", lambda e: e.memset(states_f[:, :], 0.0),
                         writes=[B_stf])
                else:
                    drain("vector", states_f[0:64, 2044:2048], B_stf)
                    S.dma("sync", coll_in, states_f[:, :], reads=[B_stf],
                          writes=[B_coll])

                    def em_coll(e):
                        return e.collective_compute(
                            "AllReduce", ALU.add,
                            replica_groups=[[0, 1], [2, 3], [4, 5], [6, 7]],
                            ins=[coll_in.opt()], outs=[coll_out.opt()])
                    S.add("gpsimd", em_coll, reads=[B_coll], writes=[B_coll],
                          kind="coll")
                    def em_neg(e):
                        return e.tensor_scalar_mul(states_f[:, :], states_f[:, :], -1.0)
                    comp("vector", em_neg, reads=[B_stf], writes=[B_stf])
                    S.dma("gpsimd", states_f[:, :], coll_out, accum_op=ALU.add,
                          reads=[B_coll], writes=[B_stf])

                def em_cb0(e):
                    return e.tensor_copy(states_b[:, :], states_f[:, :])
                comp("vector", em_cb0, reads=[B_stf], writes=[B_stb])

                halo_col = 0 if not rev else SEG + 128
                halo_take = slice(125, 128) if not rev else slice(0, 3)
                tail_dst = 0 if not rev else HS
                hx, bhx = xn_hs[0], B_xn_hs[0]
                src = bass.AP(xn_dram.tensor, halo_col,
                              [[8 * XN_STR, 128], [XN_STR, 8], [1, 128]])
                S.dma("sync", hx[:, 0:1024].rearrange("p (k t) -> p k t", k=8), src,
                      reads=[B_xn_dram], writes=[bhx])
                for j in range(16):
                    jj = 16 + j
                    for k in range(8):
                        def em_mm(e, jj=jj, k=k, hx=hx):
                            return e.matmul(
                                ps_proj[:, 0:128],
                                Win_bf[:, DPL * k + 128 * jj:DPL * k + 128 * jj + 128],
                                hx[:, 128 * k:128 * k + 128],
                                start=(k == 0), stop=(k == 7))
                        comp("tensor", em_mm, reads=[B_Win, bhx], writes=[B_proj[0]])

                    def em_ev(e, j=j, tail_dst=tail_dst, halo_take=halo_take):
                        dst = xbc_xs[:, (HS + 3) * j + tail_dst:(HS + 3) * j + tail_dst + 3]
                        if HALO_ZERO_HACK:
                            return e.memset(dst, 0.0)
                        return e.tensor_copy(dst, ps_proj[:, halo_take])
                    comp("vector", em_ev, reads=[B_proj[0]], writes=[B_xbc])
                for (colbase, dstb, bb) in ((4096, xbc_B, B_xbcB), (4160, xbc_C, B_xbcC)):
                    for k in range(8):
                        def em_mm(e, colbase=colbase, k=k, hx=hx):
                            return e.matmul(
                                ps_proj[0:64, 0:128],
                                Win_bf[:, DPL * k + colbase:DPL * k + colbase + 64],
                                hx[:, 128 * k:128 * k + 128],
                                start=(k == 0), stop=(k == 7))
                        comp("tensor", em_mm, reads=[B_Win, bhx], writes=[B_proj[0]])

                    def em_ev(e, dstb=dstb, bb=bb, tail_dst=tail_dst,
                              halo_take=halo_take):
                        if HALO_ZERO_HACK:
                            return e.memset(dstb[:, tail_dst:tail_dst + 3], 0.0)
                        return e.tensor_copy(dstb[:, tail_dst:tail_dst + 3],
                                             ps_proj[0:64, halo_take])
                    comp("vector", em_ev, reads=[B_proj[0]], writes=[bb])

                nhs = NHS if HS_LIMIT is None else HS_LIMIT
                hs_order = list(range(nhs)) if not rev else list(range(NHS - 1,
                                                                       NHS - 1 - nhs, -1))
                first = True
                raw_off = 3 if not rev else 0
                for hi, hs_i in enumerate(hs_order):
                    t0 = HS * hs_i
                    par = hi % 2
                    hx, bhx = xn_hs[par], B_xn_hs[par]
                    src = bass.AP(xn_dram.tensor, 128 + t0,
                                  [[8 * XN_STR, 128], [XN_STR, 8], [1, HS]])
                    S.dma("sync", hx[:, :].rearrange("p (k t) -> p k t", k=8), src,
                          reads=[B_xn_dram], writes=[bhx])

                    tail_src = slice(HS, HS + 3) if not rev else slice(0, 3)
                    tail_slot = slice(0, 3) if not rev else slice(HS, HS + 3)
                    if not first:
                        def em_t1(e, tail_slot=tail_slot):
                            v = xbc_xs[:, :].rearrange("p (j c) -> p j c", j=16)
                            return e.tensor_copy(
                                v[:, :, tail_slot],
                                xtail[:, :].rearrange("p (j c) -> p j c", j=16))
                        comp("vector", em_t1, reads=[B_xtail], writes=[B_xbc])

                        def em_t2(e, tail_slot=tail_slot):
                            return e.tensor_copy(xbc_B[:, tail_slot], btail[:, 0:3])
                        comp("vector", em_t2, reads=[B_btail], writes=[B_xbcB])

                        def em_t3(e, tail_slot=tail_slot):
                            return e.tensor_copy(xbc_C[:, tail_slot], btail[:, 3:6])
                        comp("vector", em_t3, reads=[B_btail], writes=[B_xbcC])
                    first = False

                    for j in range(33):
                        pp = ps_proj if (j % 2 == 0) else ps_proj2
                        bpp = B_proj[j % 2]
                        if j < 32:
                            m0, mwid = 128 * j, 128
                        else:
                            m0, mwid = 4096, 64
                        for k in range(8):
                            def em_mm(e, k=k, m0=m0, mwid=mwid, pp=pp, hx=hx):
                                return e.matmul(
                                    pp[0:mwid, 0:HS],
                                    Win_bf[:, DPL * k + m0:DPL * k + m0 + mwid],
                                    hx[:, HS * k:HS * k + HS],
                                    start=(k == 0), stop=(k == 7))
                            comp("tensor", em_mm, reads=[B_Win, bhx], writes=[bpp])
                        if j == 32:
                            for k in range(8):
                                def em_mm(e, k=k, hx=hx):
                                    return e.matmul(
                                        ps_proj2[0:64, 0:HS],
                                        Win_bf[:, DPL * k + 4160:DPL * k + 4160 + 64],
                                        hx[:, HS * k:HS * k + HS],
                                        start=(k == 0), stop=(k == 7))
                                comp("tensor", em_mm, reads=[B_Win, bhx],
                                     writes=[B_proj[1]])

                            def em_evB(e, raw_off=raw_off):
                                return e.tensor_copy(xbc_B[:, raw_off:raw_off + HS],
                                                     ps_proj[0:64, 0:HS])
                            comp("vector", em_evB, reads=[B_proj[0]], writes=[B_xbcB])

                            def em_evC(e, raw_off=raw_off):
                                return e.tensor_copy(xbc_C[:, raw_off:raw_off + HS],
                                                     ps_proj2[0:64, 0:HS])
                            comp("vector", em_evC, reads=[B_proj[1]], writes=[B_xbcC])
                        elif j < 16:
                            def em_ev(e, j=j, pp=pp):
                                return e.activation(zs_hs[:, HS * j:HS * (j + 1)],
                                                    pp[:, 0:HS], AF.Silu)
                            comp("scalar", em_ev, reads=[bpp], writes=[B_zs])
                        else:
                            jx = j - 16

                            def em_ev(e, jx=jx, pp=pp, raw_off=raw_off):
                                return e.tensor_copy(
                                    xbc_xs[:, (HS + 3) * jx + raw_off:
                                           (HS + 3) * jx + raw_off + HS],
                                    pp[:, 0:HS])
                            comp("vector", em_ev, reads=[bpp], writes=[B_xbc])

                    for j in range(16):
                        base = (HS + 3) * j
                        for k in range(4):
                            off = k if not rev else 3 - k

                            def em_c(e, j=j, k=k, off=off, base=base, p=p):
                                win = xbc_xs[:, base + off:base + off + HS]
                                wcol = wxs[p][:, 4 * j + k:4 * j + k + 1]
                                if k == 0:
                                    return e.tensor_scalar_mul(convacc[:, :], win, wcol)
                                return e.scalar_tensor_tensor(
                                    convacc[:, :], win, wcol, convacc[:, :],
                                    ALU.mult, ALU.add)
                            rds = [B_xbc, B_consts] + ([B_cacc] if k else [])
                            comp("vector", em_c, reads=rds, writes=[B_cacc])

                        def em_s(e, j=j, p=p):
                            return e.activation(cxs[:, HS * j:HS * (j + 1)],
                                                convacc[:, :], AF.Silu,
                                                bias=bxs[p][:, j:j + 1])
                        comp("scalar", em_s, reads=[B_cacc, B_consts], writes=[B_cxs])
                    for (srcb, bs, dstb, bd, wofs, bofs) in (
                            (xbc_B, B_xbcB, cB, B_cB, 0, 0),
                            (xbc_C, B_xbcC, cC, B_cC, 4, 1)):
                        for k in range(4):
                            off = k if not rev else 3 - k

                            def em_c(e, srcb=srcb, k=k, off=off, wofs=wofs, p=p):
                                win = srcb[:, off:off + HS]
                                wcol = wBC[p][:, wofs + k:wofs + k + 1]
                                if k == 0:
                                    return e.tensor_scalar_mul(convacc[0:64, :], win,
                                                               wcol)
                                return e.scalar_tensor_tensor(
                                    convacc[0:64, :], win, wcol, convacc[0:64, :],
                                    ALU.mult, ALU.add)
                            rds = [bs, B_consts] + ([B_cacc] if k else [])
                            comp("vector", em_c, reads=rds, writes=[B_cacc])

                        def em_s(e, dstb=dstb, bofs=bofs, p=p):
                            return e.activation(dstb[:, :], convacc[0:64, :], AF.Silu,
                                                bias=bBC[p][:, bofs:bofs + 1])
                        comp("scalar", em_s, reads=[B_cacc, B_consts], writes=[bd])

                    def em_sp1(e, tail_src=tail_src):
                        v = xbc_xs[:, :].rearrange("p (j c) -> p j c", j=16)
                        return e.tensor_copy(
                            xtail[:, :].rearrange("p (j c) -> p j c", j=16),
                            v[:, :, tail_src])
                    comp("vector", em_sp1, reads=[B_xbc], writes=[B_xtail])

                    def em_sp2(e, tail_src=tail_src):
                        return e.tensor_copy(btail[:, 0:3], xbc_B[:, tail_src])
                    comp("vector", em_sp2, reads=[B_xbcB], writes=[B_btail])

                    def em_sp3(e, tail_src=tail_src):
                        return e.tensor_copy(btail[:, 3:6], xbc_C[:, tail_src])
                    comp("vector", em_sp3, reads=[B_xbcC], writes=[B_btail])

                    if dpass == TAP_PASS and hs_i == TAP_HS:
                        tap("cxs", cxs[:, :], B_cxs)
                        tap("zs", zs_hs[:, :], B_zs)
                        tap("cB", cB[:, :], B_cB)
                        tap("cC", cC[:, :], B_cC)

                    ch_order = (0, 1) if not rev else (1, 0)
                    for c in ch_order:
                        kg = 2 * hs_i + c
                        tg = 128 * kg
                        cin = 128 * c
                        cpar = (hi * 2 + (0 if c == ch_order[0] else 1)) % 2
                        small_src, B_ssrc = small_srcs[cpar], B_ssrcs[cpar]
                        small_sb, B_ssb = small_sbs[cpar], B_ssbs[cpar]
                        dtw_bf, B_dtw = dtw_bfs[cpar], B_dtws[cpar]
                        dec64, B_dec64 = dec64_, B_dec64_

                        src = bass.AP(scd_dram[p].tensor, tg, [[SEG, 96], [1, 128]])
                        S.dma("sync", small_src[0:96, :], src, reads=[B_scd[p]],
                              writes=[B_ssrc])

                        def em_st(e, small_src=small_src):
                            return e.transpose(ps_gs[:, :], small_src[:, :], id_f[:, :])
                        comp("tensor", em_st, reads=[B_ssrc, B_consts], writes=[B_gs])

                        def em_se(e, small_sb=small_sb):
                            return e.tensor_copy(small_sb[:, :], ps_gs[:, 0:96])
                        comp("vector", em_se, reads=[B_gs], writes=[B_ssb])

                        def em_sc(e, dtw_bf=dtw_bf, small_sb=small_sb):
                            return e.tensor_copy(dtw_bf[:, :], small_sb[:, 32:96])
                        comp("vector", em_sc, reads=[B_ssb], writes=[B_dtw])
                        if dpass == TAP_PASS and hs_i == TAP_HS and c == TAP_CHUNK:
                            tap("smallsb", small_sb[:, :], B_ssb)

                        src = bass.AP(dec_dram[p].tensor, 32 * kg, [[0, 64], [1, 32]])
                        S.dma("sync", dec64[:, :], src, reads=[B_dec[p]],
                              writes=[B_dec64])

                        def em_g(e, cin=cin):
                            return e.matmul(ps_gs[:, :], cB[:, cin:cin + 128],
                                            cC[:, cin:cin + 128], start=True, stop=True)
                        comp("tensor", em_g, reads=[B_cB, B_cC, B_gs], writes=[B_gs])

                        def em_ge(e):
                            return e.tensor_copy(G_sb[:, :], ps_gs[:, :])
                        comp("vector", em_ge, reads=[B_gs], writes=[B_G])
                        if dpass == TAP_PASS and hs_i == TAP_HS and c == TAP_CHUNK:
                            tap("Gsb", G_sb[:, :], B_G)

                        def em_b0(e, cin=cin):
                            return e.tensor_copy(Bstage[0:64, :], cB[:, cin:cin + 128])
                        comp("vector", em_b0, reads=[B_cB], writes=[B_Bst])

                        def em_bt(e):
                            return e.transpose(ps_t[:, 0:128], Bstage[:, :], id_b[:, :])
                        comp("tensor", em_bt, reads=[B_Bst, B_consts], writes=[B_t])

                        def em_be(e):
                            return e.tensor_copy(B_tm[:, :], ps_t[:, 0:64])
                        comp("vector", em_be, reads=[B_t], writes=[B_Btm])
                        if dpass == TAP_PASS and hs_i == TAP_HS and c == TAP_CHUNK:
                            tap("Btm", B_tm[:, :], B_Btm)

                        for g in range(4):
                            for kk in range(4):
                                j = 4 * g + kk

                                def em_tp(e, j=j, kk=kk, cin=cin):
                                    return e.transpose(
                                        ps_t[:, 128 * kk:128 * kk + 128],
                                        cxs[:, HS * j + cin:HS * j + cin + 128],
                                        id_b[:, :])
                                comp("tensor", em_tp, reads=[B_cxs, B_consts],
                                     writes=[B_t])

                            def em_ev(e, g=g):
                                return e.tensor_copy(xs_tm[:, 512 * g:512 * g + 512],
                                                     ps_t[:, :])
                            comp("vector", em_ev, reads=[B_t], writes=[B_xstm])

                        if dpass == TAP_PASS and hs_i == TAP_HS and c == TAP_CHUNK:
                            tap("xstm", xs_tm[:, :], B_xstm)

                        def em_xt(e, dtw_bf=dtw_bf):
                            dt_b = dtw_bf[:, 0:32].unsqueeze(2).broadcast_to(
                                [128, 32, 64])
                            return e.tensor_tensor(
                                Xt[:, :].rearrange("p (h q) -> p h q", h=32),
                                xs_tm[:, :].rearrange("p (h q) -> p h q", h=32),
                                dt_b, ALU.mult)
                        comp("vector", em_xt, reads=[B_xstm, B_dtw], writes=[B_Xt])

                        def em_xh(e, dtw_bf=dtw_bf):
                            w_b = dtw_bf[:, 32:64].unsqueeze(2).broadcast_to(
                                [128, 32, 64])
                            return e.tensor_tensor(
                                Xh[:, :].rearrange("p (h q) -> p h q", h=32),
                                Xt[:, :].rearrange("p (h q) -> p h q", h=32),
                                w_b, ALU.mult)
                        comp("vector", em_xh, reads=[B_Xt, B_dtw], writes=[B_Xh])
                        if dpass == TAP_PASS and hs_i == TAP_HS and c == TAP_CHUNK:
                            tap("Xtd", Xt[:, :], B_Xt)
                            tap("Xhd", Xh[:, :], B_Xh)

                        for q in range(8):
                            qpar = q % 2
                            CT, B_CT = CTs[qpar], B_CTs[qpar]
                            mb_src = bass.AP(MB_d[p].tensor, 0,
                                             [[128, 128], [0, 4], [1, 128]])
                            S.dma("sync", CT[:, :].rearrange("p (h t) -> p h t", h=4),
                                  mb_src, reads=[B_consts], writes=[B_CT])
                            cs_src = bass.AP(scd_dram[p].tensor, 4 * q * SEG + tg,
                                             [[0, 128], [SEG, 4], [1, 128]])
                            S.dma("gpsimd", CT[:, :].rearrange("p (h t) -> p h t", h=4),
                                  cs_src, accum_op=ALU.add, reads=[B_scd[p]],
                                  writes=[B_CT])

                            def em_m(e, q=q, CT=CT, small_sb=small_sb):
                                cs_b = small_sb[:, 4 * q:4 * q + 4].unsqueeze(2) \
                                    .broadcast_to([128, 4, 128])
                                v = CT[:, :].rearrange("p (h t) -> p h t", h=4)
                                return e.tensor_tensor(v, v, cs_b, ALU.subtract)
                            comp("vector", em_m, reads=[B_CT, B_ssb], writes=[B_CT])
                            if dpass == TAP_PASS and hs_i == TAP_HS and c == TAP_CHUNK and q == TAP_Q:
                                tap("CTd", CT[:, :], B_CT)

                            def em_e(e, CT=CT):
                                return e.activation(E_sb[:, :], CT[:, :], AF.Exp)
                            comp("scalar", em_e, reads=[B_CT], writes=[B_E])

                            def em_sq(e):
                                g_b = G_sb[:, :].unsqueeze(1).broadcast_to([128, 4, 128])
                                return e.tensor_tensor(
                                    S_sb[:, :].rearrange("p (h t) -> p h t", h=4),
                                    E_sb[:, :].rearrange("p (h t) -> p h t", h=4),
                                    g_b, ALU.mult)
                            comp("vector", em_sq, reads=[B_E, B_G], writes=[B_S])
                            if dpass == TAP_PASS and hs_i == TAP_HS and c == TAP_CHUNK and q == TAP_Q:
                                tap("Ssb", S_sb[:, :], B_S)

                            u_src = bass.AP(u_dram[p].tensor, 4 * q * SEG + tg,
                                            [[0, 64], [SEG, 4], [1, 128]])
                            S.dma("gpsimd", u64[:, :].rearrange("p (h t) -> p h t", h=4),
                                  u_src, reads=[B_u[p]], writes=[B_u64])

                            def em_ch(e, cin=cin):
                                c_b = cC[:, cin:cin + 128].unsqueeze(1).broadcast_to(
                                    [64, 4, 128])
                                return e.tensor_tensor(
                                    Chat[:, :].rearrange("p (h t) -> p h t", h=4),
                                    c_b,
                                    u64[:, :].rearrange("p (h t) -> p h t", h=4),
                                    ALU.mult)
                            comp("vector", em_ch, reads=[B_cC, B_u64], writes=[B_Chat])
                            if dpass == TAP_PASS and hs_i == TAP_HS and c == TAP_CHUNK and q == TAP_Q:
                                tap("u64d", u64[:, :], B_u64)
                                tap("Chatd", Chat[:, :], B_Chat)

                            for hh in range(4):
                                h = 4 * q + hh
                                j = h // 2
                                parq = h % 2
                                bank = B_main[j // 4]
                                yv = ps_main[64 * parq:64 * parq + 64,
                                             128 * j:128 * j + 128]

                                def em_yi(e, h=h, hh=hh, yv=yv, parq=parq):
                                    return e.matmul(yv, Xt[:, 64 * h:64 * h + 64],
                                                    S_sb[:, 128 * hh:128 * hh + 128],
                                                    start=True, stop=False,
                                                    tile_position=(0, 64 * parq))
                                comp("tensor", em_yi, reads=[B_Xt, B_S], writes=[bank])

                                def em_ye(e, h=h, hh=hh, yv=yv, parq=parq):
                                    return e.matmul(yv,
                                                    states_b[:, 64 * h:64 * h + 64],
                                                    Chat[:, 128 * hh:128 * hh + 128],
                                                    start=False, stop=True,
                                                    tile_position=(0, 64 * parq))
                                comp("tensor", em_ye, reads=[B_stb, B_Chat],
                                     writes=[bank])

                        for q in range(4):
                            def em_cs(e, q=q):
                                return e.matmul(ps_aux[0:64, :], B_tm[:, :],
                                                Xh[:, 512 * q:512 * q + 512],
                                                start=True, stop=True)
                            comp("tensor", em_cs, reads=[B_Btm, B_Xh, B_aux],
                                 writes=[B_aux])

                            def em_u1(e, q=q, dec64=dec64):
                                d_b = dec64[:, 8 * q:8 * q + 8].unsqueeze(2) \
                                    .broadcast_to([64, 8, 64])
                                sl = states_f[:, 512 * q:512 * q + 512].rearrange(
                                    "p (h v) -> p h v", h=8)
                                return e.tensor_tensor(sl, sl, d_b, ALU.mult)
                            comp("vector", em_u1, reads=[B_stf, B_dec64],
                                 writes=[B_stf])

                            def em_u2(e, q=q):
                                sl = states_f[:, 512 * q:512 * q + 512]
                                return e.tensor_add(sl, sl, ps_aux[0:64, :])
                            comp("vector", em_u2, reads=[B_stf, B_aux], writes=[B_stf])

                        def em_cb(e):
                            return e.tensor_copy(states_b[:, :], states_f[:, :])
                        comp("vector", em_cb, reads=[B_stf], writes=[B_stb])
                        if dpass == TAP_PASS and hs_i == TAP_HS and c == TAP_CHUNK:
                            tap("states0", states_f[:, :], B_stf)
                            tap("dec64d", dec64[:, :], B_dec64)

                        for q in range(4):
                            bank = B_main[q]
                            yq = ps_main[:, 512 * q:512 * q + 512]

                            def em_dx(e, q=q, cin=cin, p=p):
                                xsf = cxs[:, :].rearrange("p (j c) -> p j c", j=16)[
                                    :, 4 * q:4 * q + 4, cin:cin + 128]
                                d_b = D16[p][:, 4 * q:4 * q + 4].unsqueeze(2) \
                                    .broadcast_to([128, 4, 128])
                                return e.tensor_tensor(
                                    dxs[:, :].rearrange("p (j c) -> p j c", j=4),
                                    xsf, d_b, ALU.mult)
                            comp("vector", em_dx, reads=[B_cxs, B_consts],
                                 writes=[B_dxs])

                            def em_y1(e, yq=yq):
                                return e.tensor_add(y1[:, :], dxs[:, :], yq)
                            comp("vector", em_y1, reads=[B_dxs, bank], writes=[B_y1])
                            if dpass == TAP_PASS and hs_i == TAP_HS and c == TAP_CHUNK and q == TAP_Q:
                                tap("y1d", y1[:, :], B_y1)
                                tap("dxsd", dxs[:, :], B_dxs)

                            def em_yg(e, q=q, cin=cin):
                                zf = zs_hs[:, :].rearrange("p (j c) -> p j c", j=16)[
                                    :, 4 * q:4 * q + 4, cin:cin + 128]
                                ygv = yg_hs[:, :].rearrange("p (j c) -> p j c", j=16)[
                                    :, 4 * q:4 * q + 4, cin:cin + 128]
                                return e.tensor_tensor(
                                    ygv, y1[:, :].rearrange("p (j c) -> p j c", j=4),
                                    zf, ALU.mult)
                            comp("vector", em_yg, reads=[B_y1, B_zs], writes=[B_yg])

                            def em_sq2(e, q=q, cin=cin):
                                ygv = yg_hs[:, :].rearrange("p (j c) -> p j c", j=16)[
                                    :, 4 * q:4 * q + 4, cin:cin + 128]
                                return e.activation(
                                    ySq[:, :].rearrange("p (j c) -> p j c", j=4),
                                    ygv, AF.Square)
                            comp("scalar", em_sq2, reads=[B_yg], writes=[B_ySq])

                            for jj in range(4):
                                def em_rm(e, jj=jj, q=q):
                                    return e.matmul(ps_aux[64:65, 0:128],
                                                    ones_col[:, :],
                                                    ySq[:, 128 * jj:128 * jj + 128],
                                                    start=(q == 0 and jj == 0),
                                                    stop=(q == 3 and jj == 3),
                                                    tile_position=(0, 64))
                                comp("tensor", em_rm, reads=[B_ySq, B_consts, B_aux],
                                     writes=[B_aux])

                        def em_rs(e, cin=cin):
                            return e.activation(rstd_c[0:1, cin:cin + 128],
                                                ps_aux[64:65, 0:128], AF.Sqrt,
                                                scale=1.0 / DI, bias=1e-5)
                        comp("scalar", em_rs, reads=[B_aux], writes=[B_rstd])

                        def em_rr(e, cin=cin):
                            return e.reciprocal(rstd_c[0:1, cin:cin + 128],
                                                rstd_c[0:1, cin:cin + 128])
                        comp("vector", em_rr, reads=[B_rstd], writes=[B_rstd])

                    if dpass == TAP_PASS and hs_i == TAP_HS:
                        tap("yg", yg_hs[:, :], B_yg)
                        tap("rstdc", rstd_c[:, :], B_rstd)

                    def em_rb(e):
                        return e.tensor_copy(rstd_b[:, :], rstd_c[:, :])
                    comp("vector", em_rb, reads=[B_rstd], writes=[B_rstdb], fence=True)

                    def em_rx(e):
                        return e.matmul(ps_aux[:, 0:HS], ones_row[:, :], rstd_b[:, :],
                                        start=True, stop=True)
                    comp("tensor", em_rx, reads=[B_rstdb, B_aux], writes=[B_aux])

                    def em_rxe(e):
                        return e.tensor_copy(rstdx[:, :], ps_aux[:, 0:HS])
                    comp("vector", em_rxe, reads=[B_aux], writes=[B_rstdx])

                    for m in range(8):
                        for k in range(16):
                            def em_mm(e, m=m, k=k):
                                return e.matmul(
                                    ps_main[:, HS * m:HS * (m + 1)],
                                    Wout_bf[:, D * k + 128 * m:D * k + 128 * m + 128],
                                    yg_hs[:, HS * k:HS * (k + 1)],
                                    start=(k == 0), stop=(k == 15))
                            comp("tensor", em_mm,
                                 reads=[B_Wout, B_yg, B_main[m // 2]],
                                 writes=[B_main[m // 2]])

                    def em_pe(e):
                        r_b = rstdx[:, :].unsqueeze(1).broadcast_to([128, 8, HS])
                        return e.tensor_tensor(
                            P_sb[:, :].rearrange("p (m t) -> p m t", m=8),
                            ps_main[:, 0:8 * HS].rearrange("p (m t) -> p m t", m=8),
                            r_b, ALU.mult)
                    comp("vector", em_pe, reads=[B_rstdx] + B_main, writes=[B_P])
                    drain("vector", P_sb[:, 8 * HS - 4:8 * HS], B_P)

                    if dpass == 1:
                        dstp = bass.AP(P_dram.tensor, t0,
                                       [[8 * SEG, 128], [SEG, 8], [1, HS]])
                        S.dma("sync", dstp,
                              P_sb[:, :].rearrange("p (m t) -> p m t", m=8),
                              reads=[B_P], writes=[B_P_dram])
                    else:
                        srcp = bass.AP(P_dram.tensor, t0,
                                       [[8 * SEG, 128], [SEG, 8], [1, HS]])
                        S.dma("sync",
                              P1_sb[:, :].rearrange("p (m t) -> p m t", m=8), srcp,
                              reads=[B_P_dram], writes=[B_P1])
                        for m in range(8):
                            for k in range(16):
                                kk = k % 8
                                rhsb, rb = (P1_sb, B_P1) if k < 8 else (P_sb, B_P)

                                def em_mm(e, m=m, k=k, kk=kk, rhsb=rhsb):
                                    return e.matmul(
                                        ps_main[:, HS * m:HS * (m + 1)],
                                        Wo_bf[:, D * k + 128 * m:D * k + 128 * m + 128],
                                        rhsb[:, HS * kk:HS * (kk + 1)],
                                        start=(k == 0), stop=(k == 15))
                                comp("tensor", em_mm,
                                     reads=[B_Wo, rb, B_main[m // 2]],
                                     writes=[B_main[m // 2]])
                        for m in range(8):
                            def em_sb(e, m=m):
                                return e.add(s_sb[:, HS * m:HS * (m + 1)],
                                             ps_main[:, HS * m:HS * (m + 1)],
                                             boc_sb[:, m:m + 1])
                            comp("scalar", em_sb, reads=[B_main[m // 2], B_consts],
                                 writes=[B_s])
                        drain("scalar", s_sb[:, 8 * HS - 4:8 * HS], B_s)
                        dsts = bass.AP(s_dram.tensor, t0,
                                       [[8 * SEG, 128], [SEG, 8], [1, HS]])
                        S.dma("sync", dsts,
                              s_sb[:, :].rearrange("p (m t) -> p m t", m=8),
                              reads=[B_s], writes=[B_s_dram])

                if dpass == 1 and "P1" in taps:
                    S.dma("sync", taps["P1"], P_dram, reads=[B_P_dram])
            if "s" in taps:
                S.dma("sync", taps["s"], s_dram, reads=[B_s_dram])
            S.barrier(allbufs)

        with contextlib.ExitStack() as fst:
            fent = fst.enter_context
            w1s = fent(nc.sbuf_tensor("w1s", [128, 8 * FH], BF16))
            B_w1s = mkbuf("w1s")
            w2s = fent(nc.sbuf_tensor("w2s", [128, 32 * D], BF16))
            B_w2s = mkbuf("w2s")
            wstage = fent(nc.sbuf_tensor("wstage", [128, FH], BF16))
            B_wst = mkbuf("wstage")
            arow_st = fent(nc.sbuf_tensor("arow_st", [1, 512], F32))
            B_arow = mkbuf("arow_st")
            abanks = [ps_main[0:1, 512 * jj:512 * jj + 512] for jj in range(4)] + \
                     [ps_proj[0:1, :], ps_proj2[0:1, :], ps_aux[0:1, :],
                      ps_tg[0:1, 0:512]]
            B_abanks = [B_main[0], B_main[1], B_main[2], B_main[3],
                        B_proj[0], B_proj[1], B_aux, B_t]
            a1c = fent(nc.sbuf_tensor("a1c", [128, 32], F32))
            B_a1c = mkbuf("a1c")
            a2c = fent(nc.sbuf_tensor("a2c", [128, 8], F32))
            B_a2c = mkbuf("a2c")
            b1c = fent(nc.sbuf_tensor("b1c_sb", [128, 32], F32))
            b2c = fent(nc.sbuf_tensor("b2c_sb", [128, 8], F32))
            B_bc = mkbuf("bc")
            xn_f = [fent(nc.sbuf_tensor(f"xn_f{i}", [128, 8 * HS], BF16))
                    for i in range(2)]
            B_xn_f = [mkbuf(f"xn_f{i}") for i in range(2)]
            h1g = fent(nc.sbuf_tensor("h1g", [128, 32 * HS], BF16))
            B_h1g = mkbuf("h1g")
            g_bf = fent(nc.sbuf_tensor("g_bf", [128, 8 * HS], BF16))
            B_g = mkbuf("g_bf")
            s_f = fent(nc.sbuf_tensor("s_f", [128, 8 * HS], BF16))
            B_sf = mkbuf("s_f")
            sg = fent(nc.sbuf_tensor("sg", [128, 8 * HS], BF16))
            B_sg = mkbuf("sg")
            x_f = [fent(nc.sbuf_tensor(f"x_f{i}", [128, D], F32)) for i in range(2)]
            B_x_f = [mkbuf(f"x_f{i}") for i in range(2)]
            o_f = fent(nc.sbuf_tensor("o_f", [128, D], F32))
            B_of = mkbuf("o_f")

            S.dma("sync", b1c[:, :], b1c_d, writes=[B_bc])
            S.dma("sync", b2c[:, :], b2c_d, writes=[B_bc])

            for k in range(8):
                S.dma("gpsimd", wstage[:, :], w1t_d[128 * k:128 * (k + 1), :],
                      writes=[B_wst])

                def em_ab(e, k=k):
                    return e.activation(w1s[:, FH * k:FH * (k + 1)], wstage[:, :],
                                        AF.Abs)
                comp("scalar", em_ab, reads=[B_wst], writes=[B_w1s])
                for nslc in range(8):
                    def em_mm(e, nslc=nslc, k=k):
                        return e.matmul(abanks[nslc], ones_col[:, :],
                                        w1s[:, FH * k + 512 * nslc:
                                            FH * k + 512 * (nslc + 1)],
                                        start=(k == 0), stop=(k == 7))
                    comp("tensor", em_mm, reads=[B_w1s, B_consts, B_abanks[nslc]],
                         writes=[B_abanks[nslc]])

                def em_sg_(e, k=k):
                    return e.sign(w1s[:, FH * k:FH * (k + 1)], wstage[:, :])
                comp("scalar", em_sg_, reads=[B_wst, B_w1s], writes=[B_w1s])
            for nslc in range(8):
                def em_ae(e, nslc=nslc):
                    return e.mul(arow_st[:, :], abanks[nslc], 1.0 / D)
                comp("scalar", em_ae, reads=[B_abanks[nslc]], writes=[B_arow])
                drain("scalar", arow_st[0:1, 508:512], B_arow)
                S.dma("sync",
                      bass.AP(alpha_dram.tensor, 512 * nslc, [[1, 1], [1, 512]]),
                      arow_st[:, :], reads=[B_arow], writes=[B_alpha_d])
            a_src = bass.AP(alpha_dram.tensor, 0, [[1, 128], [128, 32]])
            S.dma("sync", a1c[:, :], a_src, reads=[B_alpha_d], writes=[B_a1c],
                  allow_slow_non_contiguous=True)

            for k in range(32):
                S.dma("gpsimd", wstage[:, 0:D], w2t_d[128 * k:128 * (k + 1), :],
                      writes=[B_wst])

                def em_ab(e, k=k):
                    return e.activation(w2s[:, D * k:D * (k + 1)], wstage[:, 0:D],
                                        AF.Abs)
                comp("scalar", em_ab, reads=[B_wst], writes=[B_w2s])
                for nslc in range(2):
                    def em_mm(e, nslc=nslc, k=k):
                        return e.matmul(abanks[nslc], ones_col[:, :],
                                        w2s[:, D * k + 512 * nslc:
                                            D * k + 512 * (nslc + 1)],
                                        start=(k == 0), stop=(k == 31))
                    comp("tensor", em_mm, reads=[B_w2s, B_consts, B_abanks[nslc]],
                         writes=[B_abanks[nslc]])

                def em_sg_(e, k=k):
                    return e.sign(w2s[:, D * k:D * (k + 1)], wstage[:, 0:D])
                comp("scalar", em_sg_, reads=[B_wst, B_w2s], writes=[B_w2s])
            for nslc in range(2):
                def em_ae(e, nslc=nslc):
                    return e.mul(arow_st[:, :], abanks[nslc], 1.0 / FH)
                comp("scalar", em_ae, reads=[B_abanks[nslc]], writes=[B_arow])
                drain("scalar", arow_st[0:1, 508:512], B_arow)
                S.dma("sync",
                      bass.AP(alpha_dram.tensor, 512 * nslc, [[1, 1], [1, 512]]),
                      arow_st[:, :], reads=[B_arow], writes=[B_alpha_d])
            a_src = bass.AP(alpha_dram.tensor, 0, [[1, 128], [128, 8]])
            S.dma("sync", a2c[:, :], a_src, reads=[B_alpha_d], writes=[B_a2c],
                  allow_slow_non_contiguous=True)

            nhs_f = NHS if HS_LIMIT is None else HS_LIMIT
            for hi in range(nhs_f):
                t0 = HS * hi
                par = hi % 2
                xnf, bxnf = xn_f[par], B_xn_f[par]
                src = bass.AP(xn_dram.tensor, 128 + t0,
                              [[8 * XN_STR, 128], [XN_STR, 8], [1, HS]])
                S.dma("sync", xnf[:, :].rearrange("p (k t) -> p k t", k=8), src,
                      reads=[B_xn_dram], writes=[bxnf])
                srcs = bass.AP(s_dram.tensor, t0, [[8 * SEG, 128], [SEG, 8], [1, HS]])
                S.dma("sync", s_f[:, :].rearrange("p (m t) -> p m t", m=8), srcs,
                      reads=[B_s_dram], writes=[B_sf])

                for m in range(32):
                    pp = ps_proj if m % 2 == 0 else ps_proj2
                    bpp = B_proj[m % 2]
                    for k in range(8):
                        def em_mm(e, m=m, k=k, pp=pp, xnf=xnf):
                            return e.matmul(
                                pp[:, 0:HS],
                                w1s[:, FH * k + 128 * m:FH * k + 128 * m + 128],
                                xnf[:, HS * k:HS * (k + 1)],
                                start=(k == 0), stop=(k == 7))
                        comp("tensor", em_mm, reads=[B_w1s, bxnf, bpp], writes=[bpp])

                    def em_ev(e, m=m, pp=pp):
                        return e.activation(h1g[:, HS * m:HS * (m + 1)], pp[:, 0:HS],
                                            AF.Gelu, bias=b1c[:, m:m + 1],
                                            scale=a1c[:, m:m + 1])
                    comp("scalar", em_ev, reads=[bpp, B_bc, B_a1c], writes=[B_h1g])

                for m in range(8):
                    pp = ps_proj if m % 2 == 0 else ps_proj2
                    bpp = B_proj[m % 2]
                    for k in range(32):
                        def em_mm(e, m=m, k=k, pp=pp):
                            return e.matmul(
                                pp[:, 0:HS],
                                w2s[:, D * k + 128 * m:D * k + 128 * m + 128],
                                h1g[:, HS * k:HS * (k + 1)],
                                start=(k == 0), stop=(k == 31))
                        comp("tensor", em_mm, reads=[B_w2s, B_h1g, bpp], writes=[bpp])

                    def em_ev(e, m=m, pp=pp):
                        return e.activation(g_bf[:, HS * m:HS * (m + 1)], pp[:, 0:HS],
                                            AF.Silu, bias=b2c[:, m:m + 1],
                                            scale=a2c[:, m:m + 1])
                    comp("scalar", em_ev, reads=[bpp, B_bc, B_a2c], writes=[B_g])

                def em_sg2(e):
                    return e.tensor_mul(sg[:, :], g_bf[:, :], s_f[:, :])
                comp("vector", em_sg2, reads=[B_g, B_sf], writes=[B_sg])

                for c in range(2):
                    xf, bxf = x_f[c], B_x_f[c]
                    S.dma("sync", xf[:, :],
                          x_d[128 + t0 + 128 * c:128 + t0 + 128 * c + 128, :],
                          writes=[bxf])
                    for g2 in range(2):
                        for kk in range(4):
                            m = 4 * g2 + kk

                            def em_tp(e, m=m, kk=kk, c=c):
                                return e.transpose(
                                    ps_t[:, 128 * kk:128 * kk + 128],
                                    sg[:, HS * m + 128 * c:HS * m + 128 * c + 128],
                                    id_b[:, :])
                            comp("tensor", em_tp, reads=[B_sg, B_consts], writes=[B_t])

                        def em_o(e, g2=g2, xf=xf):
                            return e.tensor_add(o_f[:, 512 * g2:512 * (g2 + 1)],
                                                xf[:, 512 * g2:512 * (g2 + 1)],
                                                ps_t[:, :])
                        comp("vector", em_o, reads=[B_t, bxf], writes=[B_of])
                    drain("vector", o_f[:, D - 4:D], B_of)
                    S.dma("sync", out_d[t0 + 128 * c:t0 + 128 * c + 128, :], o_f[:, :],
                          reads=[B_of], writes=[B_out])

            S.barrier(allbufs)

        S.finalize()
        blk = ent(nc.Block())
        for ename, meth in (("tensor", blk.tensor), ("vector", blk.vector),
                            ("scalar", blk.scalar), ("sync", blk.sync),
                            ("gpsimd", blk.gpsimd)):
            def body(eng, ename=ename):
                S.stream(ename, eng, sems)
            meth(body)

    return nc


def _host_inputs(inputs):
    f32 = np.float32
    x = np.asarray(inputs["L0_attn_x"], f32)
    B = x.shape[0]
    shared = {}
    shared["w1t"] = np.ascontiguousarray(inputs["L0_ffn_w1"].T, f32)
    shared["w2t"] = np.ascontiguousarray(inputs["L0_ffn_w2"].T, f32)
    shared["b1c"] = np.ascontiguousarray(np.asarray(inputs["L0_ffn_b1"], f32)
                                         .reshape(32, 128).T)
    shared["b2c"] = np.ascontiguousarray(np.asarray(inputs["L0_ffn_b2"], f32)
                                         .reshape(8, 128).T)
    shared["boc"] = np.ascontiguousarray(np.asarray(inputs["L0_attn_bo"], f32)
                                         .reshape(8, 128).T)
    shared["ident"] = np.eye(128, dtype=f32)
    si = np.arange(128)[:, None]
    ti = np.arange(128)[None, :]
    shared["MB1"] = np.where(si <= ti, 0.0, NEG).astype(f32)
    shared["MB2"] = np.where(si >= ti, 0.0, NEG).astype(f32)
    shared["fconsts"] = np.ascontiguousarray(
        np.broadcast_to(np.asarray(FCONSTS, f32)[None, :], (128, len(FCONSTS))))
    Wo = np.asarray(inputs["L0_attn_Wo"], f32)

    def dirset(pref):
        d = {}
        d["Win"] = np.ascontiguousarray(np.asarray(inputs[f"L0_attn_{pref}Win"], f32))
        d["Wout"] = np.ascontiguousarray(np.asarray(inputs[f"L0_attn_{pref}Wout"], f32))
        cw = np.asarray(inputs[f"L0_attn_{pref}convw"], f32)[:, 0, :]
        d["wxs"] = np.ascontiguousarray(
            cw[:, :2048].reshape(4, 16, 128).transpose(2, 1, 0).reshape(128, 64))
        d["wBC"] = np.ascontiguousarray(
            np.concatenate([cw[:, 2048:2112].T, cw[:, 2112:2176].T], 1))
        cb = np.asarray(inputs[f"L0_attn_{pref}convb"], f32)
        d["bxs"] = np.ascontiguousarray(cb[:2048].reshape(16, 128).T)
        d["bBC"] = np.ascontiguousarray(np.stack([cb[2048:2112], cb[2112:2176]], 1))
        d["dtb"] = np.asarray(inputs[f"L0_attn_{pref}dtb"], f32).reshape(32, 1).copy()
        d["Alog"] = np.asarray(inputs[f"L0_attn_{pref}Alog"], f32).reshape(32, 1).copy()
        Dv = np.asarray(inputs[f"L0_attn_{pref}D"], f32)
        d16 = np.empty((128, 16), f32)
        for par in range(2):
            for j in range(16):
                d16[64 * par:64 * par + 64, j] = Dv[2 * j + par]
        d["D16"] = d16
        nw = np.asarray(inputs[f"L0_attn_{pref}normw"], f32)
        d["normw"] = np.ascontiguousarray(nw.reshape(16, 128).T)
        return d

    fset, bset = dirset("f"), dirset("b")
    Wo_f = np.ascontiguousarray(Wo[:D])
    Wo_b = np.ascontiguousarray(Wo[D:])
    zeros128 = np.zeros((128, D), f32)

    in_maps = []
    for c in range(8):
        b, half = c // 2, c % 2
        m = dict(shared)
        if half == 0:
            xp = np.concatenate([zeros128, x[b][:SEG + 128]], 0)
            p1, p2 = fset, bset
            m["Wo1"], m["Wo2"] = Wo_f, Wo_b
        else:
            xr = x[b][::-1]
            xp = np.concatenate([zeros128, xr[:SEG + 128]], 0)
            p1, p2 = bset, fset
            m["Wo1"], m["Wo2"] = Wo_b, Wo_f
        m["x"] = np.ascontiguousarray(xp)
        for key, val in p1.items():
            m["p1" + key] = val
        for key, val in p2.items():
            m["p2" + key] = val
        in_maps.append(m)
    return in_maps, B


def kernel(**inputs):
    in_maps, B = _host_inputs(inputs)
    nc = build_program()
    res = run_bass_kernel_spmd(nc, in_maps, core_ids=list(range(8)))
    out = np.empty((B, L, D), np.float32)
    for c in range(8):
        b, half = c // 2, c % 2
        o = np.asarray(res.results[c]["out"], np.float32)
        if half == 0:
            out[b, :SEG] = o
        else:
            out[b, SEG:] = o[::-1]
    return out
